# version 15
# baseline (speedup 1.0000x reference)
import numpy as np
import ml_dtypes
from contextlib import ExitStack
import concourse.bass as bass
import concourse.mybir as mybir
from concourse.bass_utils import run_bass_kernel_spmd

F32 = mybir.dt.float32
BF16 = mybir.dt.bfloat16
I32 = mybir.dt.int32
AF = mybir.ActivationFunctionType
ALU = mybir.AluOpType
AX = mybir.AxisListType

B_, T, D = 4, 2048, 2048
H, N, HD = 32, 128, 64
INNER = 8192
TOWN = 1024
NCORES = 8
SQD = float(np.sqrt(D))
SQI = float(np.sqrt(INNER))


class Builder:
    ENGS = ("sync", "scalar", "vector", "tensor", "gpsimd")

    def __init__(self, nc):
        self.nc = nc
        self.q = {e: [] for e in self.ENGS}
        self.sem = {}
        self.cnt = {}
        self.waited = {e: {} for e in self.ENGS}
        self.bufs = {}
        self.block = None

    def make_sems(self, ctx):
        self._ctx = ctx
        for e in self.ENGS:
            self.sem[e] = ctx.enter_context(self.nc.semaphore(f"s_{e}"))
            self.cnt[e] = 0

    def dma_sem(self, slot):
        name = "d_" + slot
        if name not in self.sem:
            self.sem[name] = self._ctx.enter_context(self.nc.semaphore(name))
            self.cnt[name] = 0
        return name

    def _bkey(self, b):
        if isinstance(b, tuple):
            return (id(b[0]), b[1])
        return (id(b), None)

    def op(self, eng, fn, reads=(), writes=(), inc=1, slot=None):
        if slot is not None:
            semname = self.dma_sem(slot)
            inc = 16
        else:
            semname = eng
        waits = {}
        def need(sem_name, val):
            if val > waits.get(sem_name, 0):
                waits[sem_name] = val
        for b in reads:
            st = self.bufs.get(self._bkey(b))
            if st and st["w"] is not None:
                need(*st["w"])
        for b in writes:
            st = self.bufs.get(self._bkey(b))
            if st:
                if st["w"] is not None:
                    need(*st["w"])
                for r in st["r"]:
                    need(*r)
        w2 = {}
        for s, v in waits.items():
            if s == eng:
                v = min(v, self.cnt[eng])
                if v <= 0:
                    continue
            if self.waited[eng].get(s, 0) < v:
                w2[s] = v
                self.waited[eng][s] = v
        self.q[eng].append((w2, fn, self.sem[semname], inc))
        self.cnt[semname] += inc
        mycnt = (semname, self.cnt[semname] + (1 if inc == 0 else 0))
        for b in writes:
            self.bufs[self._bkey(b)] = {"w": mycnt, "r": []}
        for b in reads:
            st = self.bufs.setdefault(self._bkey(b), {"w": None, "r": []})
            st["r"].append(mycnt)
        return mycnt

    def barrier(self):
        snap = dict(self.cnt)
        for e in self.ENGS:
            w2 = {}
            for s, v in snap.items():
                if v and self.waited[e].get(s, 0) < v:
                    w2[s] = v
                    self.waited[e][s] = v
            if w2:
                self.q[e].append((w2, None, None, 0))

    def flush(self):
        for e in self.ENGS:
            items = self.q[e]
            if not items:
                continue
            self.q[e] = []
            dec = getattr(self.block, e)
            def mk(items_):
                def emit(eng):
                    for waits, fn, sem, inc_n in items_:
                        for s, v in waits.items():
                            eng.wait_ge(self.sem[s], v)
                        if fn is not None:
                            inst = fn(eng)
                            if inc_n:
                                inst.then_inc(sem, inc_n)
                return emit
            dec(mk(items))

    def cut(self):
        self.barrier()
        self.flush()


SY, SC, VE, TE, GP = "sync", "scalar", "vector", "tensor", "gpsimd"


def qcol(t, i):
    return t[:, i:i + 1]


def build_program(debug=False, upto=99):
    nc = bass.Bass("TRN2", target_bir_lowering=False)

    x_hbm = nc.dram_tensor("x_sh", [T, D], F32, kind="ExternalInput").ap()
    maskB_hbm = nc.dram_tensor("maskB", [H, T], F32, kind="ExternalInput").ap()
    Wx_hbm = nc.dram_tensor("Wx", [D, 2 * N + H], F32, kind="ExternalInput").ap()
    Wg_hbm = nc.dram_tensor("Wg", [D, D], F32, kind="ExternalInput").ap()
    bgc_hbm = nc.dram_tensor("bgc", [128, 16], F32, kind="ExternalInput").ap()
    Wo_hbm = nc.dram_tensor("Wo", [D, D], F32, kind="ExternalInput").ap()
    At_hbm = nc.dram_tensor("At", [N, H], F32, kind="ExternalInput").ap()
    Dpc_hbm = nc.dram_tensor("Dpc", [128, 16], F32, kind="ExternalInput").ap()
    w1T_hbm = nc.dram_tensor("w1T", [D, INNER], F32, kind="ExternalInput").ap()
    b1B_hbm = nc.dram_tensor("b1B", [128, 64], F32, kind="ExternalInput").ap()
    w2T_hbm = nc.dram_tensor("w2T", [INNER, D], F32, kind="ExternalInput").ap()
    b2B_hbm = nc.dram_tensor("b2B", [128, D], F32, kind="ExternalInput").ap()
    Est_hbm = nc.dram_tensor("Est", [H, 128, 32], F32, kind="ExternalInput").ap()
    sel_hbm = nc.dram_tensor("selst", [16, 64, 128], BF16, kind="ExternalInput").ap()
    bsel_hbm = nc.dram_tensor("bsel", [H, 32, 128], BF16, kind="ExternalInput").ap()
    eyeb_hbm = nc.dram_tensor("eyeb", [128, 128], BF16, kind="ExternalInput").ap()
    eyef_hbm = nc.dram_tensor("eyef", [128, 128], F32, kind="ExternalInput").ap()
    epsc_hbm = nc.dram_tensor("epsc", [128, 8], F32, kind="ExternalInput").ap()

    out_hbm = nc.dram_tensor("out", [TOWN, D], F32, kind="ExternalOutput").ap()

    s1T_hbm = nc.dram_tensor("s1Tsc", [64, 128, INNER], BF16, kind="Internal").ap()
    hg_hbm = nc.dram_tensor("hgsc", [INNER, TOWN], F32, kind="Internal").ap()
    k2T_hbm = nc.dram_tensor("k2Tsc", [INNER, TOWN], BF16, kind="Internal").ap()
    s_hbm = nc.dram_tensor("ssc", [TOWN, D], F32, kind="Internal").ap()

    dbg = {}
    if debug:
        for name, shape, dt in [("xnT", [D, T], BF16), ("k1T", [D, TOWN], BF16),
                                ("Bfm", [N, T], F32), ("Cfm", [N, T], F32),
                                ("db", [32, T], F32), ("y", [32, TOWN], F32),
                                ("yfull", [D, TOWN], BF16), ("s", [TOWN, D], F32),
                                ("alpha", [1, 4], F32), ("S1", [INNER, TOWN], F32),
                                ("k2T", [INNER, TOWN], BF16), ("gfm", [TOWN, D], F32)]:
            dbg[name] = nc.dram_tensor("d_" + name, shape, dt, kind="ExternalOutput").ap()

    with ExitStack() as octx:
        bld = Builder(nc)
        bld.make_sems(octx)
        sb = lambda nm, shape, dt: octx.enter_context(nc.sbuf_tensor(nm, shape, dt))
        psd = lambda nm, fr: octx.enter_context(nc.psum_tensor(nm, [128, fr], F32))

        cols = sb("cols", [128, 64], F32)
        qsm = sb("qsm", [128, 128], F32)
        consts = sb("consts", [128, 128], BF16)
        eyef = sb("eyef_t", [128, 128], F32)
        At = sb("At_t", [128, 32], F32)
        Dpc = sb("Dpc_t", [128, 16], F32)
        bgc = sb("bgc_t", [128, 16], F32)
        Est = sb("Est_t", [128, H * 32], F32)
        selst = sb("selst_t", [64, 16 * 128], BF16)
        bsel = sb("bsel_t", [32, H * 128], BF16)
        alphas = sb("alphas", [128, 96], F32)
        onesf = sb("onesf", [128, 128], F32)
        ybuf = sb("ybuf", [128, 2 * TOWN], F32)
        yhl = sb("yhl", [128, TOWN], BF16)
        a1col = sb("a1col", [128, 24], F32)
        aw1 = sb("aw1", [128, 128], F32)
        epsc = sb("epsc_t", [128, 8], F32)

        pmA = psd("pmA", 2048)
        pmB = psd("pmB", 512)
        pmC = octx.enter_context(nc.psum_tensor("pmC", [128, 3072], BF16))

        block = octx.enter_context(nc.Block())
        bld.block = block

        env = dict(locals())

        bld.op(SY, lambda e: e.dma_start(out=At[:N, :H], in_=At_hbm), writes=[At], slot='At')
        bld.op(SY, lambda e: e.dma_start(out=Dpc[:], in_=Dpc_hbm), writes=[Dpc], slot='Dpc')
        bld.op(SY, lambda e: e.dma_start(out=bgc[:], in_=bgc_hbm), writes=[bgc], slot='bgc')
        bld.op(SY, lambda e: e.dma_start(out=eyef[:], in_=eyef_hbm), writes=[eyef], slot='eyef')
        bld.op(SY, lambda e: e.dma_start(out=consts[:], in_=eyeb_hbm), writes=[consts], slot='eyeb')
        bld.op(SY, lambda e: e.dma_start(out=Est.rearrange("p (h c) -> p h c", h=H), in_=Est_hbm.rearrange("h p c -> p h c")),
               writes=[Est], slot='Est')
        bld.op(SY, lambda e: e.dma_start(out=selst.rearrange("k (c p) -> k c p", c=16), in_=sel_hbm.rearrange("c k p -> k c p")),
               writes=[selst], slot='selst')
        bld.op(SY, lambda e: e.dma_start(out=bsel.rearrange("k (h p) -> k h p", h=H), in_=bsel_hbm.rearrange("h k p -> k h p")),
               writes=[bsel], slot='bsel')
        bld.op(SY, lambda e: e.dma_start(out=epsc[:], in_=epsc_hbm), writes=[epsc], slot='epsc')
        bld.op(VE, lambda e: e.memset(onesf[:], 1.0), writes=[onesf])

        with ExitStack() as sK:
            k1T = sK.enter_context(nc.sbuf_tensor("k1T", [128, 16 * TOWN], BF16))
            env["k1T"] = k1T
            with ExitStack() as sW:
                xnT_ow = sW.enter_context(nc.sbuf_tensor("xnT_ow", [128, 16 * TOWN], BF16))
                env["xnT_ow"] = xnT_ow
                with ExitStack() as sC:
                    Cfm = sC.enter_context(nc.sbuf_tensor("Cfm", [128, T], F32))
                    eB = sC.enter_context(nc.sbuf_tensor("eB", [128, T], F32))
                    ehl = sC.enter_context(nc.sbuf_tensor("ehl", [128, 2 * T], BF16))
                    env.update(Cfm=Cfm, eB=eB, ehl=ehl)
                    with ExitStack() as sO:
                        xnT_ot = sO.enter_context(
                            nc.sbuf_tensor("xnT_ot", [128, 16 * TOWN], BF16))
                        env["xnT_ot"] = xnT_ot
                        _p12(nc, bld, dbg, debug, env)
                        bld.cut()
                        if upto <= 2:
                            return nc, dbg
                        _p3(nc, bld, dbg, debug, env)
                        bld.cut()
                        if upto <= 3:
                            return nc, dbg
                    with ExitStack() as s45:
                        w1sl = [s45.enter_context(nc.sbuf_tensor(f"p45_w{i}", [128, 2048], BF16))
                                for i in range(4)]
                        s1dmp = s45.enter_context(nc.sbuf_tensor("p45_dmp", [128, 1024], BF16))
                        env.update(w1sl=w1sl, s1dmp=s1dmp)
                        _p4(nc, bld, dbg, debug, env)
                        _p45(nc, bld, dbg, debug, env)
                        bld.cut()
                    if upto <= 4:
                        return nc, dbg
                _p5(nc, bld, dbg, debug, env)
                bld.cut()
                if upto <= 5:
                    return nc, dbg
            _p6(nc, bld, dbg, debug, env)
            bld.cut()
            if upto <= 6:
                return nc, dbg
        _p8(nc, bld, dbg, debug, env)
        bld.cut()
    return nc, dbg


def xnT_reads(L, which):
    t = L["xnT_ot"] if which == "ot" else L["xnT_ow"]
    return [(t, f"{k}_{q}") for k in range(16) for q in range(4)]


def _p12(nc, bld, dbg, debug, L):
    x_hbm, cols, qsm, consts = L["x_hbm"], L["cols"], L["qsm"], L["consts"]
    epsc = L["epsc"]
    xnT_ot, xnT_ow, k1T, pmC = L["xnT_ot"], L["xnT_ow"], L["k1T"], L["pmC"]
    pm = [pmC[:, 0:256], pmC[:, 1024:1280], pmC[:, 2048:2304]]
    with ExitStack() as ctx:
        xb = ctx.enter_context(nc.sbuf_tensor("p2_xb", [128, 2 * D], F32))
        xnb = ctx.enter_context(nc.sbuf_tensor("p2_xnb", [128, 2 * D], BF16))
        k1b = ctx.enter_context(nc.sbuf_tensor("p2_k1b", [128, 2 * D], BF16))
        wk = ctx.enter_context(nc.sbuf_tensor("p2_wk", [128, D], F32))
        intb = ctx.enter_context(nc.sbuf_tensor("p2_int", [128, D], I32))

        for grp in range(8):
            tiles = [grp * 2, grp * 2 + 1]
            g0 = grp * 2
            for j, i in enumerate(tiles):
                bld.op(SY, lambda e, i=i, j=j: e.dma_start(
                    out=xb[:, j * D:(j + 1) * D], in_=x_hbm[i * 128:(i + 1) * 128, :]),
                    writes=[(xb, j)], slot=f'xb{j}')
            srd = []
            for j, i in enumerate(tiles):
                xa = xb[:, j * D:(j + 1) * D]
                bld.op(SC, lambda e, xa=xa, i=i: e.activation(
                    out=wk[:], in_=xa, func=AF.Square, accum_out=cols[:, 16 + i:17 + i]),
                    reads=[(xb, j)], writes=[(cols, f"ss{i}"), (wk, 0)])
                bld.op(VE, lambda e, xa=xa, i=i: e.reduce_sum(
                    out=cols[:, i:i + 1], in_=xa, axis=AX.X),
                    reads=[(xb, j)], writes=[(cols, f"s{i}")])
                srd += [(cols, f"s{i}"), (cols, f"ss{i}")]
                if i >= 8:
                    bld.op(VE, lambda e, xa=xa, i=i: e.reduce_max(
                        out=cols[:, 24 + i:25 + i], in_=xa, axis=AX.X),
                        reads=[(xb, j)], writes=[(cols, f"mx{i}")])
                    bld.op(VE, lambda e, xa=xa, i=i: e.tensor_reduce(
                        out=cols[:, 32 + i:33 + i], in_=xa, axis=AX.X, op=ALU.min),
                        reads=[(xb, j)], writes=[(cols, f"mn{i}")])
                    srd += [(cols, f"mx{i}"), (cols, f"mn{i}")]
            bld.op(VE, lambda e, g0=g0: e.tensor_scalar(
                out=qsm[:, 0:2], in0=cols[:, g0:g0 + 2], scalar1=1.0 / D,
                scalar2=None, op0=ALU.mult), reads=srd, writes=[(qsm, "m")])
            bld.op(SC, lambda e: e.activation(out=qsm[:, 2:4], in_=qsm[:, 0:2], func=AF.Square),
                   reads=[(qsm, "m")], writes=[(qsm, "m2")])
            bld.op(VE, lambda e, g0=g0: e.scalar_tensor_tensor(
                out=qsm[:, 4:6], in0=cols[:, 16 + g0:18 + g0], scalar=1.0 / D,
                in1=qsm[:, 2:4], op0=ALU.mult, op1=ALU.subtract),
                reads=[(qsm, "m2")] + srd, writes=[(qsm, "var")])
            bld.op(SC, lambda e: e.activation(out=qsm[:, 6:8], in_=qsm[:, 4:6],
                                              func=AF.Sqrt, bias=epsc[:, 0:1]),
                   reads=[(qsm, "var"), epsc], writes=[(qsm, "sd")])
            bld.op(VE, lambda e, g0=g0: e.reciprocal(out=qsm[:, 16 + g0:18 + g0], in_=qsm[:, 6:8]),
                   reads=[(qsm, "sd")], writes=[(qsm, f"rstd{grp}")])
            bld.op(VE, lambda e, g0=g0: e.scalar_tensor_tensor(
                out=qsm[:, 32 + g0:34 + g0], in0=qsm[:, 0:2], scalar=-1.0,
                in1=qsm[:, 16 + g0:18 + g0], op0=ALU.mult, op1=ALU.mult),
                reads=[(qsm, "m"), (qsm, f"rstd{grp}")], writes=[(qsm, f"nmr{grp}")])
            if grp >= 4:
                j0 = (grp - 4) * 2
                rstd2 = qsm[:, 16 + g0:18 + g0]
                rsr = [(qsm, f"rstd{grp}"), (qsm, "m"), (qsm, "var")]
                bld.op(VE, lambda e, g0=g0: e.tensor_sub(
                    qsm[:, 48:50], cols[:, 24 + g0:26 + g0], qsm[:, 0:2]), reads=srd + rsr,
                    writes=[(qsm, "t1")])
                bld.op(VE, lambda e, g0=g0: e.tensor_sub(
                    qsm[:, 50:52], qsm[:, 0:2], cols[:, 32 + g0:34 + g0]), reads=srd + rsr,
                    writes=[(qsm, "t2")])
                bld.op(VE, lambda e: e.tensor_max(qsm[:, 48:50], qsm[:, 48:50], qsm[:, 50:52]),
                       reads=[(qsm, "t1"), (qsm, "t2")], writes=[(qsm, "t1")])
                bld.op(VE, lambda e, rstd2=rstd2: e.tensor_mul(qsm[:, 48:50], qsm[:, 48:50], rstd2),
                       reads=[(qsm, "t1")] + rsr, writes=[(qsm, "t1")])
                bld.op(SC, lambda e, rstd2=rstd2: e.activation(
                    out=qsm[:, 52:54], in_=rstd2, func=AF.Square), reads=rsr, writes=[(qsm, "r2")])
                bld.op(VE, lambda e: e.tensor_mul(qsm[:, 52:54], qsm[:, 52:54], qsm[:, 4:6]),
                       reads=[(qsm, "r2"), (qsm, "var")], writes=[(qsm, "r2")])
                bld.op(SC, lambda e: e.activation(out=qsm[:, 52:54], in_=qsm[:, 52:54],
                                                  func=AF.Sqrt, bias=epsc[:, 1:2]),
                       reads=[(qsm, "r2"), epsc], writes=[(qsm, "r2")])
                bld.op(VE, lambda e: e.tensor_scalar(
                    out=qsm[:, 54:56], in0=qsm[:, 52:54], scalar1=SQD, scalar2=None, op0=ALU.mult),
                    reads=[(qsm, "r2")], writes=[(qsm, "iden")])
                bld.op(VE, lambda e: e.reciprocal(out=qsm[:, 54:56], in_=qsm[:, 54:56]),
                       reads=[(qsm, "iden")], writes=[(qsm, "iden")])
                bld.op(VE, lambda e: e.tensor_mul(qsm[:, 48:50], qsm[:, 48:50], qsm[:, 54:56]),
                       reads=[(qsm, "t1"), (qsm, "iden")], writes=[(qsm, "t1")])
                bld.op(VE, lambda e: e.tensor_scalar(
                    out=qsm[:, 48:50], in0=qsm[:, 48:50], scalar1=1e-5, scalar2=None, op0=ALU.max),
                    reads=[(qsm, "t1")], writes=[(qsm, "t1")])
                bld.op(VE, lambda e, j0=j0: e.tensor_copy(qsm[:, 64 + j0:66 + j0], qsm[:, 48:50]),
                       reads=[(qsm, "t1")], writes=[(qsm, f"mm1_{j0}")])
                bld.op(VE, lambda e: e.reciprocal(out=qsm[:, 50:52], in_=qsm[:, 48:50]),
                       reads=[(qsm, "t1")], writes=[(qsm, "t2")])
                bld.op(VE, lambda e: e.tensor_scalar(
                    out=qsm[:, 50:52], in0=qsm[:, 50:52], scalar1=127.0, scalar2=None, op0=ALU.mult),
                    reads=[(qsm, "t2")], writes=[(qsm, "t2")])
                bld.op(VE, lambda e: e.tensor_mul(qsm[:, 50:52], qsm[:, 50:52], qsm[:, 54:56]),
                       reads=[(qsm, "t2"), (qsm, "iden")], writes=[(qsm, "t2")])
                bld.op(VE, lambda e, rstd2=rstd2, j0=j0: e.tensor_mul(
                    qsm[:, 72 + j0:74 + j0], qsm[:, 50:52], rstd2),
                    reads=[(qsm, "t2")] + rsr, writes=[(qsm, f"c1_{j0}")])
                bld.op(VE, lambda e, j0=j0: e.tensor_mul(
                    qsm[:, 56:58], qsm[:, 0:2], qsm[:, 72 + j0:74 + j0]),
                    reads=[(qsm, "m"), (qsm, f"c1_{j0}")], writes=[(qsm, "mc1")])
                bld.op(VE, lambda e, j0=j0: e.tensor_scalar(
                    out=qsm[:, 80 + j0:82 + j0], in0=qsm[:, 56:58], scalar1=-1.0,
                    scalar2=256.50002, op0=ALU.mult, op1=ALU.add),
                    reads=[(qsm, "mc1")], writes=[(qsm, f"bsh_{j0}")])
                bld.op(VE, lambda e, j0=j0: e.tensor_scalar(
                    out=qsm[:, 88 + j0:90 + j0], in0=qsm[:, 56:58], scalar1=-256.5,
                    scalar2=None, op0=ALU.add),
                    reads=[(qsm, "mc1")], writes=[(qsm, f"thr_{j0}")])
                bld.op(VE, lambda e, j0=j0: e.tensor_scalar(
                    out=qsm[:, 96 + j0:98 + j0], in0=qsm[:, 56:58], scalar1=-255.5,
                    scalar2=None, op0=ALU.add),
                    reads=[(qsm, "mc1")], writes=[(qsm, f"thrB_{j0}")])

            for j, i in enumerate(tiles):
                xa = xb[:, j * D:(j + 1) * D]
                bld.op(VE, lambda e, xa=xa, j=j, g0=g0: e.tensor_scalar(
                    out=xnb[:, j * D:(j + 1) * D], in0=xa,
                    scalar1=qsm[:, 16 + g0 + j:17 + g0 + j],
                    scalar2=qsm[:, 32 + g0 + j:33 + g0 + j],
                    op0=ALU.mult, op1=ALU.add),
                    reads=[(xb, j), (qsm, f"rstd{grp}"), (qsm, f"nmr{grp}")],
                    writes=[(xnb, j)])
            dst = xnT_ot if grp < 4 else xnT_ow
            doff = (grp % 4) * 256
            for k in range(16):
                pb = pm[k % 2]
                for j in range(2):
                    bld.op(TE, lambda e, k=k, j=j, pb=pb: e.transpose(
                        pb[:, j * 128:(j + 1) * 128],
                        xnb[:, j * D + k * 128:j * D + (k + 1) * 128], consts[:]),
                        reads=[(xnb, j), consts], writes=[(pmC, f"C{k % 2}")])
                eng = SC if k % 2 == 0 else VE
                def evac(e, pb=pb, k=k, dst=dst, doff=doff, eng=eng):
                    dsl = dst[:, k * TOWN + doff:k * TOWN + doff + 256]
                    if eng == SC:
                        return e.activation(out=dsl, in_=pb[:, 0:256], func=AF.Copy)
                    return e.tensor_copy(dsl, pb[:, 0:256])
                bld.op(eng, evac, reads=[(pmC, f"C{k % 2}")],
                       writes=[(dst, f"{k}_{grp % 4}")])
            if grp >= 4:
                j0 = (grp - 4) * 2
                for j, i in enumerate(tiles):
                    xa = xb[:, j * D:(j + 1) * D]
                    c1 = qsm[:, 72 + j0 + j:73 + j0 + j]
                    bsh = qsm[:, 80 + j0 + j:81 + j0 + j]
                    thr = qsm[:, 88 + j0 + j:89 + j0 + j]
                    thrB = qsm[:, 96 + j0 + j:97 + j0 + j]
                    crd = [(qsm, f"c1_{j0}"), (qsm, f"bsh_{j0}"), (qsm, f"thr_{j0}"),
                           (qsm, f"thrB_{j0}")]
                    bld.op(VE, lambda e, xa=xa, c1=c1, bsh=bsh: e.tensor_scalar(
                        out=intb[:], in0=xa, scalar1=c1, scalar2=bsh,
                        op0=ALU.mult, op1=ALU.add),
                        reads=[(xb, j)] + crd, writes=[(intb, 0)])
                    bld.op(VE, lambda e, xa=xa, c1=c1: e.scalar_tensor_tensor(
                        out=wk[:], in0=xa, scalar=c1, in1=intb[:],
                        op0=ALU.mult, op1=ALU.subtract),
                        reads=[(xb, j), (intb, 0)] + crd, writes=[(wk, 0)])
                    bld.op(VE, lambda e, thr=thr: e.tensor_scalar(
                        out=xnb[:, 0:D], in0=wk[:], scalar1=thr, scalar2=None, op0=ALU.is_gt),
                        reads=[(wk, 0)] + crd, writes=[(xnb, 0)])
                    bld.op(VE, lambda e, thrB=thrB: e.scalar_tensor_tensor(
                        out=wk[:], in0=wk[:], scalar=thrB, in1=xnb[:, 0:D],
                        op0=ALU.is_gt, op1=ALU.add),
                        reads=[(wk, 0), (xnb, 0)] + crd, writes=[(wk, 0)])
                    bld.op(VE, lambda e, j=j: e.scalar_tensor_tensor(
                        out=k1b[:, j * D:(j + 1) * D], in0=wk[:], scalar=257.0, in1=intb[:],
                        op0=ALU.subtract, op1=ALU.add),
                        reads=[(wk, 0), (intb, 0)], writes=[(k1b, j)])
                for k in range(16):
                    pb = pm[2]
                    for j in range(2):
                        bld.op(TE, lambda e, k=k, j=j, pb=pb: e.transpose(
                            pb[:, j * 128:(j + 1) * 128],
                            k1b[:, j * D + k * 128:j * D + (k + 1) * 128], consts[:]),
                            reads=[(k1b, j), consts], writes=[(pmC, "C2")])
                    bld.op(VE, lambda e, pb=pb, k=k, j0=j0: e.tensor_copy(
                        k1T[:, k * TOWN + j0 * 128:k * TOWN + (j0 + 2) * 128], pb[:, 0:256]),
                        reads=[(pmC, "C2")],
                        writes=[(k1T, f"{k}_{j0}")])
        if debug:
            for k in range(16):
                bld.op(SY, lambda e, k=k: e.dma_start(
                    out=dbg["xnT"][k * 128:(k + 1) * 128, 0:TOWN],
                    in_=xnT_ot[:, k * TOWN:(k + 1) * TOWN]),
                    reads=[(xnT_ot, f"{k}_{q}") for q in range(4)], slot='dbg1')
                bld.op(SY, lambda e, k=k: e.dma_start(
                    out=dbg["xnT"][k * 128:(k + 1) * 128, TOWN:T],
                    in_=xnT_ow[:, k * TOWN:(k + 1) * TOWN]),
                    reads=[(xnT_ow, f"{k}_{q}") for q in range(4)], slot='dbg2')
                bld.op(SY, lambda e, k=k: e.dma_start(
                    out=dbg["k1T"][k * 128:(k + 1) * 128, :],
                    in_=k1T[:, k * TOWN:(k + 1) * TOWN]),
                    reads=[(k1T, f"{k}_{j0}") for j0 in (0, 2, 4, 6)], slot='dbg3')


def _p3(nc, bld, dbg, debug, L):
    xnT_ot, xnT_ow, Cfm, eB, ehl = L["xnT_ot"], L["xnT_ow"], L["Cfm"], L["eB"], L["ehl"]
    pmA, pmB = L["pmA"], L["pmB"]
    Wx_hbm, maskB_hbm = L["Wx_hbm"], L["maskB_hbm"]
    pm = [pmA[:, i * 512:(i + 1) * 512] for i in range(4)]
    pdb = pmB[0:32, 0:512]
    with ExitStack() as ctx:
        WxB = ctx.enter_context(nc.sbuf_tensor("WxB", [128, 16 * 288], BF16))
        Bfm = ctx.enter_context(nc.sbuf_tensor("Bfm", [128, T], F32))
        dwk = ctx.enter_context(nc.sbuf_tensor("dwk", [128, 2 * T], F32))
        bld.op(GP, lambda e: e.dma_start(
            out=WxB.rearrange("p (k c) -> p k c", k=16), in_=Wx_hbm.rearrange("(k p) c -> p k c", p=128)),
            writes=[WxB], slot='WxB')
        bld.op(SY, lambda e: e.dma_start(out=dwk[:H, T:2 * T], in_=maskB_hbm),
               writes=[(dwk, "R")], slot='mask')
        for half, which in ((0, "ot"), (1, "ow")):
            src = xnT_ot if which == "ot" else xnT_ow
            rds = xnT_reads(L, which)
            for tc in range(2):
                c0 = tc * 512
                for (dst, co, tag, pb, pt) in ((Bfm, 0, "B", pm[tc % 2], f"A{tc % 2}"),
                                               (Cfm, 128, "C", pm[2 + tc % 2], f"A{2 + tc % 2}")):
                    for k in range(16):
                        bld.op(TE, lambda e, k=k, pb=pb, co=co, src=src, c0=c0: e.matmul(
                            pb[:, 0:512], WxB[:, k * 288 + co:k * 288 + co + 128],
                            src[:, k * TOWN + c0:k * TOWN + c0 + 512],
                            start=(k == 0), stop=(k == 15)),
                            reads=[WxB] + rds, writes=[(pmA, pt)], inc=(1 if k == 15 else 0))
                    bld.op(VE, lambda e, pb=pb, dst=dst, half=half, c0=c0: e.tensor_copy(
                        dst[:, half * TOWN + c0:half * TOWN + c0 + 512], pb[:, 0:512]),
                        reads=[(pmA, pt)], writes=[(dst, f"{tag}{half}{c0}")])
                for k in range(16):
                    bld.op(TE, lambda e, k=k, src=src, c0=c0: e.matmul(
                        pdb, WxB[:, k * 288 + 256:k * 288 + 288],
                        src[:, k * TOWN + c0:k * TOWN + c0 + 512],
                        start=(k == 0), stop=(k == 15)),
                        reads=[WxB] + rds, writes=[(pmB, "B0")], inc=(1 if k == 15 else 0))
                bld.op(VE, lambda e, half=half, c0=c0: e.tensor_copy(
                    dwk[:32, half * TOWN + c0:half * TOWN + c0 + 512], pdb),
                    reads=[(pmB, "B0")], writes=[(dwk, f"L{half}{c0}")])
        dbrd = [(dwk, f"L{h}{t}") for h in range(2) for t in (0, 512)]
        if debug:
            bld.op(SY, lambda e: e.dma_start(out=dbg["db"], in_=dwk[:32, 0:T]), reads=dbrd, slot="dbg4")
        bld.op(VE, lambda e: e.tensor_add(dwk[:32, T:2 * T], dwk[:32, 0:T], dwk[:32, T:2 * T]),
               reads=dbrd + [(dwk, "R")], writes=[(dwk, "R")])
        bld.op(SC, lambda e: e.activation(out=dwk[:32, T:2 * T], in_=dwk[:32, T:2 * T], func=AF.Exp),
               reads=[(dwk, "R")], writes=[(dwk, "R")])
        bld.op(VE, lambda e: e.tensor_copy(ehl[:32, 0:T], dwk[:32, T:2 * T]),
               reads=[(dwk, "R")], writes=[(ehl, "hi")])
        bld.op(VE, lambda e: e.tensor_sub(ehl[:32, T:2 * T], dwk[:32, T:2 * T], ehl[:32, 0:T]),
               reads=[(dwk, "R"), (ehl, "hi")], writes=[(ehl, "lo")])
        Brd = [(Bfm, f"B{h}{t}") for h in range(2) for t in (0, 512)]
        bld.op(SC, lambda e: e.activation(out=eB[:], in_=Bfm[:], func=AF.Exp),
               reads=Brd, writes=[eB])
        if debug:
            bld.op(SY, lambda e: e.dma_start(out=dbg["Bfm"], in_=Bfm[:]), reads=Brd, slot="dbg5")
            bld.op(SY, lambda e: e.dma_start(out=dbg["Cfm"], in_=Cfm[:]),
                   reads=[(Cfm, f"C{h}{t}") for h in range(2) for t in (0, 512)], slot="dbg6")


def _p4(nc, bld, dbg, debug, L):
    Cfm, eB, ehl, Est, At = L["Cfm"], L["eB"], L["ehl"], L["Est"], L["At"]
    bsel, pmA, pmB, ybuf, yhl = L["bsel"], L["pmA"], L["pmB"], L["ybuf"], L["yhl"]
    alphas, onesf, a1col, qsm = L["alphas"], L["onesf"], L["a1col"], L["qsm"]
    w1T_hbm, s1T_hbm = L["w1T_hbm"], L["s1T_hbm"]
    w1sl, s1dmp = L["w1sl"], L["s1dmp"]
    pwH = pmA[:, 0:1024]
    pyv = [pmA[:, 1024:1536], pmA[:, 1536:2048]]
    Crd = [(Cfm, f"C{h}{t}") for h in range(2) for t in (0, 512)]
    with ExitStack() as ctx:
        d_t = [ctx.enter_context(nc.sbuf_tensor(f"p4_d{i}", [128, T], F32)) for i in range(3)]
        a_t = [ctx.enter_context(nc.sbuf_tensor(f"p4_a{i}", [128, T], F32)) for i in range(3)]
        h_t = [ctx.enter_context(nc.sbuf_tensor(f"p4_h{i}", [128, T], F32)) for i in range(2)]
        yp_t = [ctx.enter_context(nc.sbuf_tensor(f"p4_yp{i}", [128, TOWN], F32))
                for i in range(2)]

        def tail_vg(h):
            at_, ut_, ht_, ypt = a_t[h % 3], d_t[h % 3], h_t[h % 2], yp_t[h % 2]
            bld.op(VE, lambda e, at_=at_, ut_=ut_, ht_=ht_: e.tensor_tensor_scan(
                out=ht_[:], data0=at_[:], data1=ut_[:],
                initial=0.0, op0=ALU.mult, op1=ALU.add),
                   reads=[at_, (ut_, 0), (ut_, 1)], writes=[ht_])
            bld.op(GP, lambda e, ht_=ht_, ypt=ypt: e.tensor_mul(
                ypt[:], ht_[:, TOWN:T], Cfm[:, TOWN:T]),
                   reads=[ht_] + Crd, writes=[ypt])

        def tail_y(h):
            ypt = yp_t[h % 2]
            for tc in range(2):
                bld.op(TE, lambda e, h=h, tc=tc, ypt=ypt: e.matmul(
                    pyv[tc][0:32, :], Est[:, h * 32:(h + 1) * 32],
                    ypt[:, tc * 512:(tc + 1) * 512],
                    start=(h == 0), stop=(h == H - 1)),
                    reads=[Est, ypt], writes=[(pmA, f"A{2 + tc}")])

        def w1_load(q):
            kq, ih = q // 4, q % 4
            wb = w1sl[q % 4]
            bld.op(GP, lambda e, kq=kq, ih=ih, wb=wb: e.dma_start(
                out=wb[:],
                in_=w1T_hbm[kq * 128:(kq + 1) * 128, ih * 2048:(ih + 1) * 2048]),
                writes=[(wb, 0)], slot=f'w1e{q % 4}')

        bld.op(VE, lambda e: e.memset(L["aw1"][:, 64:128], 0.0),
               writes=[(L["aw1"], "zz")])
        for q in range(2):
            w1_load(q)
        for h in range(H):
            dt_, at_ = d_t[h % 3], a_t[h % 3]
            for half in (0, 1):
                hoff = half * TOWN
                for tc in (0, 1):
                    for part in (0, 1):
                        bld.op(TE, lambda e, h=h, tc=tc, part=part, hoff=hoff: e.matmul(
                            pwH[:, tc * 512:(tc + 1) * 512],
                            bsel[:, h * 128:(h + 1) * 128],
                            ehl[:32, part * T + hoff + tc * 512:part * T + hoff + (tc + 1) * 512],
                            start=(part == 0), stop=(part == 1)),
                            reads=[bsel, (ehl, "hi"), (ehl, "lo")], writes=[(pmA, f"A{tc}")],
                            inc=(1 if part == 1 else 0))
                bld.op(VE, lambda e, hoff=hoff, dt_=dt_: e.tensor_mul(
                    dt_[:, hoff:hoff + TOWN], eB[:, hoff:hoff + TOWN], pwH[:, :]),
                    reads=[eB, (pmA, "A0"), (pmA, "A1")], writes=[(dt_, half)])
            if h > 1:
                tail_vg(h - 2)
            if h > 2:
                tail_y(h - 3)
            for q in (2 * h + 2, 2 * h + 3):
                if q < 64:
                    w1_load(q)
            for q in (2 * h, 2 * h + 1):
                kq, ih = q // 4, q % 4
                wb = w1sl[q % 4]
                aw1_ = L["aw1"]
                if q % 2:
                    for hf in range(2):
                        bld.op(SC, lambda e, wb=wb, q=q, hf=hf, aw1_=aw1_: e.activation(
                            out=s1dmp[:], in_=wb[:, hf * 1024:(hf + 1) * 1024], func=AF.Abs,
                            accum_out=aw1_[:, 64 * hf + q:64 * hf + q + 1]),
                            reads=[(wb, 0)], writes=[s1dmp, (aw1_, f"a1_{64 * hf + q}")])
                else:
                    bld.op(VE, lambda e, wb=wb, q=q, aw1_=aw1_: e.reduce_sum(
                        out=aw1_[:, q:q + 1], in_=wb[:], axis=AX.X,
                        apply_absolute_value=True),
                        reads=[(wb, 0)], writes=[(aw1_, f"a1_{q}")])
                bld.op(SC, lambda e, wb=wb: e.activation(
                    out=wb[:], in_=wb[:], func=AF.Sign),
                    reads=[(wb, 0)], writes=[(wb, 0)])
                for j in range(4):
                    bld.op(SY, lambda e, kq=kq, ih=ih, j=j, wb=wb: e.dma_start(
                        out=s1T_hbm[4 * ih + j, :, kq * 512:(kq + 1) * 512],
                        in_=wb[:, j * 512:(j + 1) * 512]),
                        reads=[(wb, 0)],
                        writes=[(s1T_hbm, f"{4 * ih + j}_{kq}")], slot=f's1Tw{(q % 4) * 4 + j}')
            bld.op(SC, lambda e, dt_=dt_: e.activation(out=dt_[:], in_=dt_[:], func=AF.Ln, bias=1.0),
                   reads=[(dt_, 0), (dt_, 1)], writes=[(dt_, 0), (dt_, 1)])
            bld.op(SC, lambda e, h=h, dt_=dt_, at_=at_: e.activation(
                out=at_[:], in_=dt_[:], func=AF.Exp, scale=At[:, h:h + 1]),
                   reads=[(dt_, 0), (dt_, 1), At], writes=[at_])
            bld.op(GP, lambda e, dt_=dt_: e.tensor_mul(
                dt_[:], dt_[:], dt_[:]),
                   reads=[(dt_, 0), (dt_, 1)], writes=[(dt_, 0), (dt_, 1)])
        tail_vg(H - 2)
        tail_y(H - 3)
        tail_vg(H - 1)
        tail_y(H - 2)
        tail_y(H - 1)
        bld.op(VE, lambda e: e.tensor_copy(ybuf[:32, 0:512], pyv[0][0:32, :]),
               reads=[(pmA, "A2")], writes=[(ybuf, "y")])
        bld.op(VE, lambda e: e.tensor_copy(ybuf[:32, 512:1024], pyv[1][0:32, :]),
               reads=[(pmA, "A3")], writes=[(ybuf, "y")])
        bld.op(VE, lambda e: e.tensor_copy(yhl[:32, :], ybuf[:32, 0:TOWN]),
               reads=[(ybuf, "y")], writes=[(yhl, "hi")])
        bld.op(VE, lambda e: e.tensor_sub(ybuf[:32, TOWN:2 * TOWN], ybuf[:32, 0:TOWN], yhl[:32, :]),
               reads=[(ybuf, "y"), (yhl, "hi")], writes=[(ybuf, "lof")])
        bld.op(GP, lambda e: e.dma_start(out=yhl[32:64, :], in_=ybuf[:32, TOWN:2 * TOWN]),
               reads=[(ybuf, "lof")], writes=[(yhl, "lo")], slot='ylo')
        if debug:
            bld.op(SY, lambda e: e.dma_start(out=dbg["y"], in_=ybuf[:32, 0:TOWN]),
                   reads=[(ybuf, "y")], slot='dbg7')


def _p45(nc, bld, dbg, debug, L):
    alphas, onesf, a1col, qsm = L["alphas"], L["onesf"], L["a1col"], L["qsm"]
    aw1 = L["aw1"]
    pmB = L["pmB"]
    ard = ([(aw1, f"a1_{oc}") for oc in range(64)]
           + [(aw1, f"a1_{64 + oc}") for oc in range(1, 64, 2)] + [(aw1, "zz")])
    bld.op(VE, lambda e: e.reduce_sum(out=alphas[:, 80:81], in_=aw1[:, 0:128], axis=AX.X),
           reads=ard, writes=[(alphas, "rs1")])
    bld.op(TE, lambda e: e.matmul(pmB[0:1, 0:1], onesf[:, 0:1], alphas[:, 80:81],
                                  start=True, stop=True),
           reads=[(alphas, "rs1"), onesf], writes=[(pmB, "B0")])
    bld.op(VE, lambda e: e.tensor_scalar(
        out=alphas[0:1, 82:83], in0=pmB[0:1, 0:1], scalar1=1.0 / (INNER * D),
        scalar2=None, op0=ALU.mult),
        reads=[(pmB, "B0")], writes=[(alphas, "al1s")])
    bld.op(TE, lambda e: e.matmul(pmB[:, 4:5], onesf[0:1, 0:128], alphas[0:1, 82:83],
                                  start=True, stop=True),
           reads=[(alphas, "al1s"), onesf], writes=[(pmB, "B0")])
    bld.op(VE, lambda e: e.tensor_scalar(
        out=a1col[:, 0:8], in0=qsm[:, 64:72], scalar1=pmB[:, 4:5], scalar2=1.0 / 127.0,
        op0=ALU.mult, op1=ALU.mult),
        reads=[(pmB, "B0")] + [(qsm, f"mm1_{j0}") for j0 in (0, 2, 4, 6)],
        writes=[(a1col, "a1")])
    if debug:
        bld.op(SY, lambda e: e.dma_start(out=dbg["alpha"][0:1, 0:1], in_=alphas[0:1, 82:83]),
               reads=[(alphas, "al1s")], slot='dbgA')


def _p5(nc, bld, dbg, debug, L):
    xnT_ow, yhl, selst, Dpc, bgc = L["xnT_ow"], L["yhl"], L["selst"], L["Dpc"], L["bgc"]
    pmA, pmB, eyef = L["pmA"], L["pmB"], L["eyef"]
    Wg_hbm, Wo_hbm, s_hbm = L["Wg_hbm"], L["Wo_hbm"], L["s_hbm"]
    pg = pmA[:, 0:1024]
    pyb = pmA[:, 1024:2048]
    ps1 = pmA[:, 0:1024]
    ptr = [pmA[:, 1024:1152], pmA[:, 1536:1664]]
    xrd = xnT_reads(L, "ow")
    with ExitStack() as ctx:
        WB = ctx.enter_context(nc.sbuf_tensor("p5_WB", [128, 16 * D], BF16))
        yfB = ctx.enter_context(nc.sbuf_tensor("p5_yfB", [128, 16 * TOWN], BF16))
        gt = ctx.enter_context(nc.sbuf_tensor("p5_gt", [128, TOWN], F32))
        sf = ctx.enter_context(nc.sbuf_tensor("p5_sf", [128, TOWN], F32))
        sx = ctx.enter_context(nc.sbuf_tensor("p5_sx", [128, TOWN], F32))
        bld.op(GP, lambda e: e.dma_start(
            out=WB.rearrange("p (k c) -> p k c", k=16), in_=Wg_hbm.rearrange("(k p) c -> p k c", p=128)),
            writes=[(WB, 0)], slot='WB')
        for c in range(16):
            for k in range(16):
                for tc in range(2):
                    bld.op(TE, lambda e, c=c, k=k, tc=tc: e.matmul(
                        pg[:, tc * 512:(tc + 1) * 512],
                        WB[:, k * D + c * 128:k * D + (c + 1) * 128],
                        xnT_ow[:, k * TOWN + tc * 512:k * TOWN + (tc + 1) * 512],
                        start=(k == 0), stop=(k == 15)),
                        reads=[(WB, 0)] + xrd, writes=[(pmA, "A0"), (pmA, "A1")],
                        inc=(1 if (k == 15 and tc == 1) else 0))
            bld.op(SC, lambda e, c=c: e.activation(
                out=gt[:], in_=pg[:, :], func=AF.Sigmoid, bias=bgc[:, c:c + 1]),
                reads=[(pmA, "A0"), (pmA, "A1")], writes=[gt])
            for tc in range(2):
                bld.op(TE, lambda e, c=c, tc=tc: e.matmul(
                    pyb[:, tc * 512:(tc + 1) * 512], selst[:, c * 128:(c + 1) * 128],
                    yhl[0:64, tc * 512:(tc + 1) * 512], start=True, stop=True),
                    reads=[selst, (yhl, "hi"), (yhl, "lo")], writes=[(pmA, f"A{2 + tc}")])
            bld.op(VE, lambda e, c=c: e.tensor_scalar(
                out=sf[:], in0=xnT_ow[:, c * TOWN:(c + 1) * TOWN],
                scalar1=Dpc[:, c:c + 1], scalar2=None, op0=ALU.mult),
                reads=xrd + [Dpc], writes=[(sf, 0)])
            bld.op(VE, lambda e: e.tensor_add(sf[:], sf[:], pyb[:, :]),
                   reads=[(sf, 0), (pmA, "A2"), (pmA, "A3")], writes=[(sf, 0)])
            bld.op(VE, lambda e, c=c: e.tensor_mul(
                yfB[:, c * TOWN:(c + 1) * TOWN], sf[:], gt[:]),
                reads=[(sf, 0), gt], writes=[(yfB, c)])
        if debug:
            for k in range(16):
                bld.op(SY, lambda e, k=k: e.dma_start(
                    out=dbg["yfull"][k * 128:(k + 1) * 128, :],
                    in_=yfB[:, k * TOWN:(k + 1) * TOWN]), reads=[(yfB, k)], slot='dbg8')
        bld.op(GP, lambda e: e.dma_start(
            out=WB.rearrange("p (k c) -> p k c", k=16), in_=Wo_hbm.rearrange("(k p) c -> p k c", p=128)),
            writes=[(WB, 0)], slot='WB')
        yfrd = [(yfB, c) for c in range(16)]
        for c in range(16):
            for k in range(16):
                for tc in range(2):
                    bld.op(TE, lambda e, c=c, k=k, tc=tc: e.matmul(
                        ps1[:, tc * 512:(tc + 1) * 512],
                        WB[:, k * D + c * 128:k * D + (c + 1) * 128],
                        yfB[:, k * TOWN + tc * 512:k * TOWN + (tc + 1) * 512],
                        start=(k == 0), stop=(k == 15)),
                        reads=[(WB, 0)] + yfrd, writes=[(pmA, "A0"), (pmA, "A1")],
                        inc=(1 if (k == 15 and tc == 1) else 0))
            bld.op(VE, lambda e: e.tensor_copy(sf[:], ps1[:, :]),
                   reads=[(pmA, "A0"), (pmA, "A1")], writes=[(sf, 0)])
            for j in range(8):
                pb = ptr[j % 2]
                bld.op(TE, lambda e, j=j, pb=pb: e.transpose(
                    pb, sf[:, j * 128:(j + 1) * 128], eyef[:]),
                    reads=[(sf, 0), eyef], writes=[(pmA, f"A{2 + j % 2}")])
                bld.op(VE, lambda e, j=j, pb=pb: e.tensor_copy(
                    sx[:, j * 128:(j + 1) * 128], pb),
                    reads=[(pmA, f"A{2 + j % 2}")], writes=[(sx, j)])
            bld.op(SY, lambda e, c=c: e.dma_start(
                out=s_hbm[0:TOWN, c * 128:(c + 1) * 128].rearrange("(j p) c -> p j c", p=128),
                in_=sx.rearrange("p (j c) -> p j c", c=128)),
                reads=[(sx, j) for j in range(8)], writes=[(s_hbm, c)], slot='swr')
            if debug:
                bld.op(SY, lambda e, c=c: e.dma_start(
                    out=dbg["s"][0:TOWN, c * 128:(c + 1) * 128].rearrange("(j p) c -> p j c", p=128),
                    in_=sx.rearrange("p (j c) -> p j c", c=128)),
                    reads=[(sx, j) for j in range(8)], slot='dbg9')


def _p6(nc, bld, dbg, debug, L):
    k1T, alphas, a1col, epsc = L["k1T"], L["alphas"], L["a1col"], L["epsc"]
    pmA, pmB = L["pmA"], L["pmB"]
    onesf, eyef = L["onesf"], L["eyef"]
    s1T_hbm, hg_hbm, k2T_hbm, b1c_hbm = (
        L["s1T_hbm"], L["hg_hbm"], L["k2T_hbm"], L["b1B_hbm"])
    pm = [pmA[:, i * 512:(i + 1) * 512] for i in range(3)]
    k1rd = [(k1T, f"{k}_{j0}") for k in range(16) for j0 in (0, 2, 4, 6)]
    with ExitStack() as ctx:
        b1c = ctx.enter_context(nc.sbuf_tensor("p6_b1c", [128, 64], F32))
        acs = ctx.enter_context(nc.sbuf_tensor("p6_acs", [128, TOWN], F32))
        acm = ctx.enter_context(nc.sbuf_tensor("p6_acm", [128, TOWN], F32))
        rows = ctx.enter_context(nc.sbuf_tensor("p6_rows", [128, 8 * TOWN], F32))
        a1b = ctx.enter_context(nc.sbuf_tensor("p6_a1b", [128, TOWN], F32))
        c2b = ctx.enter_context(nc.sbuf_tensor("p6_c2b", [128, TOWN], F32))
        hgc2 = [ctx.enter_context(nc.sbuf_tensor(f"p6_hg{i}", [128, TOWN], F32))
                for i in range(2)]
        bld.op(SY, lambda e: e.dma_start(out=b1c[:, 0:64], in_=b1c_hbm),
               writes=[b1c], slot='b1B')
        for tt in range(8):
            bk = "A0" if tt < 4 else "A1"
            bld.op(TE, lambda e, tt=tt: e.matmul(
                pmA[0:1, tt * 128:(tt + 1) * 128], a1col[:, tt:tt + 1],
                eyef[:, 0:128], start=True, stop=True),
                reads=[(a1col, "a1"), eyef], writes=[(pmA, bk)])
        bld.op(VE, lambda e: e.tensor_copy(rows[0:1, 0 * TOWN:1 * TOWN], pmA[0:1, 0:TOWN]),
               reads=[(pmA, "A0"), (pmA, "A1")], writes=[(rows, "a1r")])
        for tc in range(2):
            bk = f"A{tc}"
            bld.op(TE, lambda e, tc=tc: e.matmul(
                pm[tc][:, 0:512], onesf[0:1, 0:128],
                rows[0:1, 0 * TOWN + tc * 512:0 * TOWN + (tc + 1) * 512], start=True, stop=True),
                reads=[(rows, "a1r"), onesf], writes=[(pmA, bk)])
            bld.op(VE, lambda e, tc=tc: e.tensor_copy(
                a1b[:, tc * 512:(tc + 1) * 512], pm[tc][:, 0:512]),
                reads=[(pmA, bk)], writes=[(a1b, 0)])
        with ExitStack() as cS:
            ssl2 = [cS.enter_context(nc.sbuf_tensor(f"p6_s{i}", [128, 16 * 512], BF16))
                    for i in range(2)]
            h1b2 = [cS.enter_context(nc.sbuf_tensor(f"p6_h1{i}", [128, 512], F32))
                    for i in range(2)]
            h2b2 = [cS.enter_context(nc.sbuf_tensor(f"p6_h2{i}", [128, TOWN], F32))
                    for i in range(2)]
            for oc in range(16):
                sb_ = ssl2[oc % 2]
                bld.op(SY, lambda e, oc=oc, sb_=sb_: e.dma_start(
                    out=sb_[:], in_=s1T_hbm[oc]),
                    reads=[(L["s1T_hbm"], f"{oc}_{kq}") for kq in range(16)],
                    writes=[(sb_, 0)], slot=f'w1s{oc % 2}')
                for icl in range(4):
                    ic = oc * 4 + icl
                    hgc = hgc2[ic % 2]
                    h2b = h2b2[ic % 2]
                    for tc in range(2):
                        bi = (2 * ic + tc) % 3
                        pb = pm[bi]
                        for k in range(16):
                            bld.op(TE, lambda e, k=k, icl=icl, tc=tc, pb=pb, sb_=sb_: e.matmul(
                                pb[:, 0:512],
                                sb_[:, k * 512 + icl * 128:k * 512 + (icl + 1) * 128],
                                k1T[:, k * TOWN + tc * 512:k * TOWN + (tc + 1) * 512],
                                start=(k == 0), stop=(k == 15)),
                                reads=[(sb_, 0)] + k1rd, writes=[(pmA, f"A{bi}")],
                                inc=(1 if k == 15 else 0))
                        h1 = h1b2[tc]
                        bld.op(VE, lambda e, pb=pb, h1=h1, tc=tc: e.tensor_mul(
                            h1[:], pb[:, 0:512], a1b[:, tc * 512:(tc + 1) * 512]),
                            reads=[(pmA, f"A{bi}"), (a1b, 0)], writes=[(h1b2[tc], 0)])
                        bld.op(SC, lambda e, h1=h1, hgc=hgc, tc=tc, ic=ic: e.activation(
                            out=hgc[:, tc * 512:(tc + 1) * 512], in_=h1[:], func=AF.Gelu,
                            bias=b1c[:, ic:ic + 1]),
                            reads=[(h1b2[tc], 0), b1c], writes=[(hgc, tc)])
                        bld.op(SC, lambda e, hgc=hgc, h2b=h2b, tc=tc: e.activation(
                            out=h2b[:, tc * 512:(tc + 1) * 512],
                            in_=hgc[:, tc * 512:(tc + 1) * 512], func=AF.Square),
                            reads=[(hgc, tc)], writes=[(h2b, tc)])
                    if ic == 0:
                        bld.op(VE, lambda e, h2b=h2b: e.tensor_copy(acs[:], h2b[:]),
                               reads=[(h2b, 0), (h2b, 1)], writes=[(acs, 0)])
                        bld.op(VE, lambda e, h2b=h2b: e.tensor_copy(acm[:], h2b[:]),
                               reads=[(h2b, 0), (h2b, 1)], writes=[(acm, 0)])
                    else:
                        bld.op(VE, lambda e, h2b=h2b: e.tensor_add(acs[:], acs[:], h2b[:]),
                               reads=[(h2b, 0), (h2b, 1), (acs, 0)], writes=[(acs, 0)])
                        bld.op(VE, lambda e, h2b=h2b: e.tensor_max(acm[:], acm[:], h2b[:]),
                               reads=[(h2b, 0), (h2b, 1), (acm, 0)], writes=[(acm, 0)])
                    bld.op(SY, lambda e, hgc=hgc, ic=ic: e.dma_start(
                        out=hg_hbm[ic * 128:(ic + 1) * 128, :], in_=hgc[:]),
                        reads=[(hgc, 0), (hgc, 1)], writes=[(hg_hbm, ic)],
                        slot=f'hgw{ic % 2}')
        qs8 = ctx.enter_context(nc.sbuf_tensor("p6_q8", [128, 32], F32))
        for tt in range(8):
            c0, c1 = 1024 + (tt % 4) * 128, 1536 + (tt % 4) * 128
            bld.op(TE, lambda e, tt=tt, c0=c0: e.transpose(
                pmA[:, c0:c0 + 128], acs[:, tt * 128:(tt + 1) * 128], eyef[:]),
                reads=[(acs, 0), eyef], writes=[(pmA, "A2")])
            bld.op(TE, lambda e, tt=tt, c1=c1: e.transpose(
                pmA[:, c1:c1 + 128], acm[:, tt * 128:(tt + 1) * 128], eyef[:]),
                reads=[(acm, 0), eyef], writes=[(pmA, "A3")])
            bld.op(VE, lambda e, tt=tt, c0=c0: e.reduce_sum(
                out=qs8[:, tt:tt + 1], in_=pmA[:, c0:c0 + 128],
                axis=AX.X), reads=[(pmA, "A2")], writes=[(qs8, f"s{tt}")])
            bld.op(VE, lambda e, tt=tt, c1=c1: e.reduce_max(
                out=qs8[:, 8 + tt:9 + tt], in_=pmA[:, c1:c1 + 128],
                axis=AX.X), reads=[(pmA, "A3")], writes=[(qs8, f"m{tt}")])
        qsrd = [(qs8, f"s{tt}") for tt in range(8)]
        qmrd = [(qs8, f"m{tt}") for tt in range(8)]
        bld.op(SC, lambda e: e.activation(
            out=qs8[:, 0:8], in_=qs8[:, 0:8], func=AF.Sqrt,
            bias=epsc[:, 1:2], scale=epsc[:, 2:3]),
            reads=qsrd + [epsc], writes=[(qs8, "rms2")])
        bld.op(SC, lambda e: e.activation(
            out=qs8[:, 8:16], in_=qs8[:, 8:16], func=AF.Sqrt),
            reads=qmrd, writes=[(qs8, "mxa")])
        bld.op(VE, lambda e: e.tensor_scalar(
            out=qs8[:, 16:24], in0=qs8[:, 0:8], scalar1=SQI, scalar2=None,
            op0=ALU.mult), reads=[(qs8, "rms2")], writes=[(qs8, "iden2")])
        bld.op(VE, lambda e: e.reciprocal(out=qs8[:, 16:24], in_=qs8[:, 16:24]),
               reads=[(qs8, "iden2")], writes=[(qs8, "iden2")])
        bld.op(VE, lambda e: e.tensor_mul(a1col[:, 8:16], qs8[:, 8:16], qs8[:, 16:24]),
               reads=[(qs8, "mxa"), (qs8, "iden2")],
               writes=[(a1col, "mm2pre")])
        bld.op(VE, lambda e: e.tensor_scalar(
            out=a1col[:, 8:16], in0=a1col[:, 8:16], scalar1=1e-5, scalar2=None,
            op0=ALU.max), reads=[(a1col, "mm2pre")],
            writes=[(a1col, f"mm2_{tt}") for tt in range(8)])
        mmrd = [(a1col, f"mm2_{tt}") for tt in range(8)]
        bld.op(VE, lambda e: e.reciprocal(out=qs8[:, 24:32], in_=a1col[:, 8:16]),
               reads=mmrd, writes=[(qs8, "c2c")])
        bld.op(VE, lambda e: e.tensor_scalar(
            out=qs8[:, 24:32], in0=qs8[:, 24:32], scalar1=127.0, scalar2=None,
            op0=ALU.mult), reads=[(qs8, "c2c")], writes=[(qs8, "c2c")])
        bld.op(VE, lambda e: e.tensor_mul(qs8[:, 24:32], qs8[:, 24:32], qs8[:, 16:24]),
               reads=[(qs8, "c2c"), (qs8, "iden2")], writes=[(qs8, "c2c")])
        for tt in range(8):
            bk = "A0" if tt < 4 else "A1"
            bld.op(TE, lambda e, tt=tt, bk=bk: e.matmul(
                pmA[0:1, tt * 128:(tt + 1) * 128], qs8[:, 24 + tt:25 + tt],
                eyef[:, 0:128], start=True, stop=True),
                reads=[(qs8, "c2c"), eyef], writes=[(pmA, bk)])
        bld.op(VE, lambda e: e.tensor_copy(rows[0:1, 7 * TOWN:8 * TOWN], pmA[0:1, 0:TOWN]),
               reads=[(pmA, "A0"), (pmA, "A1")], writes=[(rows, "c2r")])
        for tc in range(2):
            bk = f"A{tc}"
            bld.op(TE, lambda e, tc=tc, bk=bk: e.matmul(
                pm[tc][:, 0:512], onesf[0:1, 0:128],
                rows[0:1, 7 * TOWN + tc * 512:7 * TOWN + (tc + 1) * 512],
                start=True, stop=True),
                reads=[(rows, "c2r"), onesf], writes=[(pmA, bk)])
            bld.op(VE, lambda e, tc=tc: e.tensor_copy(
                c2b[:, tc * 512:(tc + 1) * 512], pm[tc][:, 0:512]),
                reads=[(pmA, bk)], writes=[(c2b, 0)])
        bld.cut()
        with ExitStack() as cR:
            hgb4 = [cR.enter_context(nc.sbuf_tensor(f"p7_hg{i}", [128, TOWN], F32))
                    for i in range(4)]
            vb2 = [cR.enter_context(nc.sbuf_tensor(f"p7_v{i}", [128, TOWN], F32))
                   for i in range(2)]
            intb2 = [cR.enter_context(nc.sbuf_tensor(f"p7_int{i}", [128, TOWN], I32))
                     for i in range(2)]
            tb2 = [cR.enter_context(nc.sbuf_tensor(f"p7_t{i}", [128, TOWN], F32))
                   for i in range(2)]
            k2c2 = [cR.enter_context(nc.sbuf_tensor(f"p7_k2{i}", [128, TOWN], BF16))
                    for i in range(2)]

            def hgr(ic):
                hgb = hgb4[ic % 4]
                bld.op(SY, lambda e, ic=ic, hgb=hgb: e.dma_start(
                    out=hgb[:], in_=hg_hbm[ic * 128:(ic + 1) * 128, :]),
                    reads=[(hg_hbm, ic)], writes=[(hgb, 0)], slot=f'hgr{ic % 4}')

            for ic in range(3):
                hgr(ic)
            for ic in range(64):
                hgb, vb = hgb4[ic % 4], vb2[ic % 2]
                intb, tb, k2c = intb2[ic % 2], tb2[ic % 2], k2c2[ic % 2]
                if debug:
                    bld.op(SY, lambda e, ic=ic, hgb=hgb: e.dma_start(
                        out=dbg["S1"][ic * 128:(ic + 1) * 128, :], in_=hgb[:]),
                        reads=[(hgb, 0)], slot=f'dbgB{ic % 2}')
                bld.op(VE, lambda e, hgb=hgb, vb=vb: e.tensor_mul(
                    vb[:], hgb[:], c2b[:]),
                    reads=[(hgb, 0), (c2b, 0)], writes=[(vb, 0)])
                bld.op(SC, lambda e, vb=vb, intb=intb: e.activation(
                    out=intb[:], in_=vb[:], func=AF.Copy, bias=256.50002),
                    reads=[(vb, 0)], writes=[(intb, 0)])
                bld.op(VE, lambda e, vb=vb, intb=intb, tb=tb: e.scalar_tensor_tensor(
                    out=tb[:], in0=vb[:], scalar=1.0, in1=intb[:],
                    op0=ALU.mult, op1=ALU.subtract),
                    reads=[(vb, 0), (intb, 0)], writes=[(tb, 0)])
                bld.op(VE, lambda e, tb=tb, vb=vb: e.tensor_scalar(
                    out=vb[:], in0=tb[:], scalar1=-256.5, scalar2=None, op0=ALU.is_gt),
                    reads=[(tb, 0)], writes=[(vb, 0)])
                bld.op(VE, lambda e, tb=tb, vb=vb: e.scalar_tensor_tensor(
                    out=tb[:], in0=tb[:], scalar=-255.5, in1=vb[:],
                    op0=ALU.is_gt, op1=ALU.add),
                    reads=[(tb, 0), (vb, 0)], writes=[(tb, 0)])
                bld.op(VE, lambda e, tb=tb, intb=intb, k2c=k2c: e.scalar_tensor_tensor(
                    out=k2c[:], in0=tb[:], scalar=257.0, in1=intb[:],
                    op0=ALU.subtract, op1=ALU.add),
                    reads=[(tb, 0), (intb, 0), (k2c, 0)], writes=[(k2c, 0)])
                if ic + 3 < 64:
                    hgr(ic + 3)
                bld.op(SY, lambda e, ic=ic, k2c=k2c: e.dma_start(
                    out=k2T_hbm[ic * 128:(ic + 1) * 128, :], in_=k2c[:]),
                    reads=[(k2c, 0)], writes=[(k2T_hbm, ic)], slot=f'k2w{ic % 2}')
                if debug:
                    bld.op(SY, lambda e, ic=ic, k2c=k2c: e.dma_start(
                        out=dbg["k2T"][ic * 128:(ic + 1) * 128, :], in_=k2c[:]),
                        reads=[(k2c, 0)], slot=f'dbgC{ic % 2}')


def _p8(nc, bld, dbg, debug, L):
    alphas, a1col, onesf, eyef, bsel = L["alphas"], L["a1col"], L["onesf"], L["eyef"], L["bsel"]
    pmA, pmB = L["pmA"], L["pmB"]
    w2T_hbm, s_hbm, k2T_hbm = L["w2T_hbm"], L["s_hbm"], L["k2T_hbm"]
    x_hbm, out_hbm, b2B_hbm = L["x_hbm"], L["out_hbm"], L["b2B_hbm"]
    GI = 8
    NG = 64 // GI
    with ExitStack() as ctx:
        S2 = ctx.enter_context(nc.sbuf_tensor("p8_S2", [128, 16 * TOWN], F32))
        ctxM = ctx.enter_context(ExitStack())
        w2b = [ctxM.enter_context(nc.sbuf_tensor(f"p8_w{i}", [128, 2048], F32)) for i in range(2)]
        s2b2 = [ctxM.enter_context(nc.sbuf_tensor(f"p8_s2_{i}", [128, GI * 2048], BF16))
                for i in range(2)]
        k2g = [ctxM.enter_context(nc.sbuf_tensor(f"p8_k2g{i}", [128, GI * TOWN], BF16))
               for i in range(2)]
        for gi in range(NG):
            kg = k2g[gi % 2]
            s2b = s2b2[gi % 2]
            bld.op(SY, lambda e, gi=gi, kg=kg: e.dma_start(
                out=kg.rearrange("p (k c) -> p k c", k=GI),
                in_=k2T_hbm[gi * GI * 128:(gi + 1) * GI * 128, :]
                .rearrange("(k p) c -> p k c", p=128)),
                writes=[(kg, 0)], reads=[(k2T_hbm, c) for c in range(gi * GI, (gi + 1) * GI)],
                slot=f'k2g{gi % 2}')
            for ic in range(GI):
                icg = gi * GI + ic
                wb = w2b[ic % 2]
                bld.op(SY, lambda e, icg=icg, wb=wb: e.dma_start(
                    out=wb[:], in_=w2T_hbm[icg * 128:(icg + 1) * 128, :]),
                    writes=[(wb, 0)], slot=f'w2s{icg % 2}')
                bld.op(SC, lambda e, wb=wb, ic=ic, s2b=s2b: e.activation(
                    out=s2b[:, ic * 2048:(ic + 1) * 2048], in_=wb[:], func=AF.Sign),
                    reads=[(wb, 0)], writes=[(s2b, ic)])
                bld.op(SC, lambda e, wb=wb, icg=icg: e.activation(
                    out=wb[:], in_=wb[:], func=AF.Abs,
                    accum_out=alphas[:, 16 + icg:17 + icg]),
                    reads=[(wb, 0)], writes=[(wb, 0), (alphas, f"w2a{icg}")])
            for ds in range(16):
                pb = pmA[:, (ds % 2) * 1024:(ds % 2 + 1) * 1024]
                for ic in range(GI):
                    for tc in range(2):
                        bld.op(TE, lambda e, ic=ic, ds=ds, tc=tc, pb=pb, kg=kg, s2b=s2b: e.matmul(
                            pb[:, tc * 512:(tc + 1) * 512],
                            s2b[:, ic * 2048 + ds * 128:ic * 2048 + (ds + 1) * 128],
                            kg[:, ic * TOWN + tc * 512:ic * TOWN + (tc + 1) * 512],
                            start=(ic == 0), stop=(ic == GI - 1)),
                            reads=[(s2b, ic), (kg, 0)],
                            writes=[(pmA, f"A{(ds % 2) * 2}"), (pmA, f"A{(ds % 2) * 2 + 1}")],
                            inc=(1 if (ic == GI - 1 and tc == 1) else 0))
                sl = S2[:, ds * TOWN:(ds + 1) * TOWN]
                prd = [(pmA, f"A{(ds % 2) * 2}"), (pmA, f"A{(ds % 2) * 2 + 1}")]
                if gi == 0:
                    bld.op(VE, lambda e, sl=sl, pb=pb: e.tensor_copy(sl, pb),
                           reads=prd, writes=[(S2, ds)])
                else:
                    bld.op(VE, lambda e, sl=sl, pb=pb: e.tensor_add(sl, sl, pb),
                           reads=prd + [(S2, ds)], writes=[(S2, ds)])
        ard = [(alphas, f"w2a{icg}") for icg in range(64)]
        bld.op(VE, lambda e: e.reduce_sum(out=alphas[:, 81:82], in_=alphas[:, 16:80], axis=AX.X),
               reads=ard, writes=[(alphas, "rs2")])
        bld.op(TE, lambda e: e.matmul(pmB[0:1, 0:1], onesf[:, 0:1], alphas[:, 81:82],
                                      start=True, stop=True),
               reads=[(alphas, "rs2"), onesf], writes=[(pmB, "B0")])
        bld.op(VE, lambda e: e.tensor_scalar(
            out=alphas[0:1, 83:84], in0=pmB[0:1, 0:1], scalar1=1.0 / (INNER * D),
            scalar2=None, op0=ALU.mult), reads=[(pmB, "B0")], writes=[(alphas, "al2s")])
        bld.op(TE, lambda e: e.matmul(pmB[:, 4:5], onesf[0:1, 0:128], alphas[0:1, 83:84],
                                      start=True, stop=True),
               reads=[(alphas, "al2s"), onesf], writes=[(pmB, "B0")])
        bld.op(VE, lambda e: e.tensor_scalar(
            out=a1col[:, 8:16], in0=a1col[:, 8:16], scalar1=pmB[:, 4:5], scalar2=1.0 / 127.0,
            op0=ALU.mult, op1=ALU.mult),
            reads=[(pmB, "B0")] + [(a1col, f"mm2_{tt}") for tt in range(8)],
            writes=[(a1col, "a2")])
        if debug:
            bld.op(SY, lambda e: e.dma_start(out=dbg["alpha"][0:1, 1:2], in_=alphas[0:1, 83:84]),
                   reads=[(alphas, "al2s")], slot='dbgD')
        bld.cut()
        ctxM.close()

        with ExitStack() as c2x:
            b2B = c2x.enter_context(nc.sbuf_tensor("p8_b2B", [128, D], F32))
            stile = c2x.enter_context(nc.sbuf_tensor("p8_st", [128, D], F32))
            xtile = c2x.enter_context(nc.sbuf_tensor("p8_xt", [128, D], F32))
            ogt = c2x.enter_context(nc.sbuf_tensor("p8_og", [128, D], F32))
            fblk = c2x.enter_context(nc.sbuf_tensor("p8_fb", [128, 256], F32))
            ptr = [pmA[:, 0:128], pmA[:, 512:640], pmA[:, 1024:1152]]
            bld.op(SY, lambda e: e.dma_start(out=b2B[:], in_=b2B_hbm), writes=[b2B], slot='b2B')
            for j in range(8):
                a2 = a1col[:, 8 + j:9 + j]
                bld.op(SY, lambda e, j=j: e.dma_start(
                    out=stile[:], in_=s_hbm[j * 128:(j + 1) * 128, :]),
                    writes=[(stile, 0)], reads=[(s_hbm, c) for c in range(16)], slot='sgr')
                bld.op(SY, lambda e, j=j: e.dma_start(
                    out=xtile[:], in_=x_hbm[TOWN + j * 128:TOWN + (j + 1) * 128, :]),
                    writes=[(xtile, 0)], slot='xtl')
                for ds in range(16):
                    pb = ptr[ds % 3]
                    bld.op(TE, lambda e, j=j, ds=ds, pb=pb: e.transpose(
                        pb, S2[:, ds * TOWN + j * 128:ds * TOWN + (j + 1) * 128], eyef[:]),
                        reads=[(S2, ds), eyef], writes=[(pmA, f"A{ds % 3}")])
                    fb = fblk[:, (ds % 2) * 128:(ds % 2 + 1) * 128]
                    bld.op(VE, lambda e, a2=a2, ds=ds, pb=pb, fb=fb: e.scalar_tensor_tensor(
                        out=fb, in0=pb, scalar=a2, in1=b2B[:, ds * 128:(ds + 1) * 128],
                        op0=ALU.mult, op1=ALU.add),
                        reads=[(pmA, f"A{ds % 3}"), b2B, (a1col, "a2")],
                        writes=[(fblk, ds % 2)])
                    bld.op(SC, lambda e, ds=ds, fb=fb: e.activation(
                        out=ogt[:, ds * 128:(ds + 1) * 128], in_=fb, func=AF.Silu),
                        reads=[(fblk, ds % 2)], writes=[(ogt, ds)])
                ogt_all = [(ogt, ds) for ds in range(16)]
                if debug:
                    bld.op(SY, lambda e, j=j: e.dma_start(
                        out=dbg["gfm"][j * 128:(j + 1) * 128, 0:D], in_=ogt[:]),
                        reads=ogt_all, slot='dbgE')
                bld.op(VE, lambda e: e.tensor_mul(ogt[:], ogt[:], stile[:]),
                       reads=[(stile, 0)] + ogt_all, writes=ogt_all)
                bld.op(VE, lambda e: e.tensor_add(ogt[:], ogt[:], xtile[:]),
                       reads=[(xtile, 0)] + ogt_all, writes=ogt_all)
                bld.op(SY, lambda e, j=j: e.dma_start(
                    out=out_hbm[j * 128:(j + 1) * 128, :], in_=ogt[:]),
                    reads=ogt_all, slot='outw')



def host_inputs(full):
    x = np.asarray(full["L0_attn_x"], np.float32)
    Wx = np.ascontiguousarray(np.asarray(full["L0_attn_Wx"], np.float32))
    Wg = np.ascontiguousarray(np.asarray(full["L0_attn_Wg"], np.float32))
    bg = np.asarray(full["L0_attn_bg"], np.float32)
    Wo = np.ascontiguousarray(np.asarray(full["L0_attn_Wo"], np.float32))
    A = np.asarray(full["L0_attn_A"], np.float32)
    Dp = np.asarray(full["L0_attn_Dp"], np.float32)
    w1 = np.asarray(full["L0_ffn_w1"], np.float32)
    b1 = np.asarray(full["L0_ffn_b1"], np.float32)
    w2 = np.asarray(full["L0_ffn_w2"], np.float32)
    b2 = np.asarray(full["L0_ffn_b2"], np.float32)

    At = np.ascontiguousarray(A.T)
    Dpc = np.ascontiguousarray(np.repeat(Dp, HD).reshape(16, 128).T)
    bgc = np.ascontiguousarray(bg.reshape(16, 128).T)
    w1T = np.ascontiguousarray(w1.T)
    w2T = np.ascontiguousarray(w2.T)
    b1B = np.ascontiguousarray(b1.reshape(64, 128).T)
    b2B = np.ascontiguousarray(np.broadcast_to(b2, (128, D)))
    Est = np.zeros((H, 128, 32), np.float32)
    for h in range(H):
        Est[h, :, h] = 1
    selst = np.zeros((16, 64, 128), ml_dtypes.bfloat16)
    for c in range(16):
        for p in range(128):
            h = (c * 128 + p) // HD
            selst[c, h % 32, p] = 1
            selst[c, 32 + h % 32, p] = 1
    bsel = np.zeros((H, 32, 128), ml_dtypes.bfloat16)
    for h in range(H):
        bsel[h, h, :] = 1
    eyeb = np.eye(128).astype(ml_dtypes.bfloat16)
    eyef = np.eye(128).astype(np.float32)
    epsc = np.ascontiguousarray(np.broadcast_to(
        np.array([1e-5, 1e-6, 1.0 / INNER, 0.5, 0.0, 0.0, 0.0, 0.0], np.float32), (128, 8)))

    shared = dict(Wx=Wx, Wg=Wg, bgc=bgc, Wo=Wo, At=At, Dpc=Dpc, w1T=w1T,
                  b1B=b1B, w2T=w2T, b2B=b2B, Est=Est, selst=selst, bsel=bsel,
                  eyeb=eyeb, eyef=eyef, epsc=epsc)
    in_maps = []
    for c in range(NCORES):
        b, gg = c // 2, c % 2
        if gg == 1:
            xs = np.ascontiguousarray(x[b])
            mb = np.zeros((H, T), np.float32)
        else:
            xs = np.ascontiguousarray(
                np.concatenate([np.zeros((TOWN, D), np.float32), x[b, :TOWN]], axis=0))
            mb = np.concatenate(
                [np.full((H, TOWN), -40.0, np.float32), np.zeros((H, TOWN), np.float32)], axis=1)
        m = dict(shared)
        m["x_sh"] = xs
        m["maskB"] = np.ascontiguousarray(mb)
        in_maps.append(m)
    return in_maps


def assemble(results):
    out = np.empty((B_, T, D), np.float32)
    for c in range(NCORES):
        b, gg = c // 2, c % 2
        out[b, gg * TOWN:(gg + 1) * TOWN, :] = results[c]["out"]
    return out


def kernel(**inputs):
    in_maps = host_inputs(inputs)
    nc, _ = build_program(debug=False)
    res = run_bass_kernel_spmd(nc, in_maps, core_ids=list(range(NCORES)))
    return assemble(res.results)


# version 16 (final)
# speedup vs baseline: 1.0329x; 1.0329x over previous
import numpy as np
import ml_dtypes
from contextlib import ExitStack
import concourse.bass as bass
import concourse.mybir as mybir
from concourse.bass_utils import run_bass_kernel_spmd

F32 = mybir.dt.float32
BF16 = mybir.dt.bfloat16
I32 = mybir.dt.int32
AF = mybir.ActivationFunctionType
ALU = mybir.AluOpType
AX = mybir.AxisListType

B_, T, D = 4, 2048, 2048
H, N, HD = 32, 128, 64
INNER = 8192
TOWN = 1024
NCORES = 8
SQD = float(np.sqrt(D))
SQI = float(np.sqrt(INNER))


class Builder:
    ENGS = ("sync", "scalar", "vector", "tensor", "gpsimd")

    def __init__(self, nc):
        self.nc = nc
        self.q = {e: [] for e in self.ENGS}
        self.sem = {}
        self.cnt = {}
        self.waited = {e: {} for e in self.ENGS}
        self.bufs = {}
        self.block = None

    def make_sems(self, ctx):
        self._ctx = ctx
        for e in self.ENGS:
            self.sem[e] = ctx.enter_context(self.nc.semaphore(f"s_{e}"))
            self.cnt[e] = 0

    def dma_sem(self, slot):
        name = "d_" + slot
        if name not in self.sem:
            self.sem[name] = self._ctx.enter_context(self.nc.semaphore(name))
            self.cnt[name] = 0
        return name

    def _bkey(self, b):
        if isinstance(b, tuple):
            return (id(b[0]), b[1])
        return (id(b), None)

    def op(self, eng, fn, reads=(), writes=(), inc=1, slot=None):
        if slot is not None:
            semname = self.dma_sem(slot)
            inc = 16
        else:
            semname = eng
        waits = {}
        def need(sem_name, val):
            if val > waits.get(sem_name, 0):
                waits[sem_name] = val
        for b in reads:
            st = self.bufs.get(self._bkey(b))
            if st and st["w"] is not None:
                need(*st["w"])
        for b in writes:
            st = self.bufs.get(self._bkey(b))
            if st:
                if st["w"] is not None:
                    need(*st["w"])
                for r in st["r"]:
                    need(*r)
        w2 = {}
        for s, v in waits.items():
            if s == eng:
                v = min(v, self.cnt[eng])
                if v <= 0:
                    continue
            if self.waited[eng].get(s, 0) < v:
                w2[s] = v
                self.waited[eng][s] = v
        self.q[eng].append((w2, fn, self.sem[semname], inc))
        self.cnt[semname] += inc
        mycnt = (semname, self.cnt[semname] + (1 if inc == 0 else 0))
        for b in writes:
            self.bufs[self._bkey(b)] = {"w": mycnt, "r": []}
        for b in reads:
            st = self.bufs.setdefault(self._bkey(b), {"w": None, "r": []})
            st["r"].append(mycnt)
        return mycnt

    def barrier(self):
        snap = dict(self.cnt)
        for e in self.ENGS:
            w2 = {}
            for s, v in snap.items():
                if v and self.waited[e].get(s, 0) < v:
                    w2[s] = v
                    self.waited[e][s] = v
            if w2:
                self.q[e].append((w2, None, None, 0))

    def flush(self):
        for e in self.ENGS:
            items = self.q[e]
            if not items:
                continue
            self.q[e] = []
            dec = getattr(self.block, e)
            def mk(items_):
                def emit(eng):
                    for waits, fn, sem, inc_n in items_:
                        for s, v in waits.items():
                            eng.wait_ge(self.sem[s], v)
                        if fn is not None:
                            inst = fn(eng)
                            if inc_n:
                                inst.then_inc(sem, inc_n)
                return emit
            dec(mk(items))

    def cut(self):
        self.barrier()
        self.flush()


SY, SC, VE, TE, GP = "sync", "scalar", "vector", "tensor", "gpsimd"


def qcol(t, i):
    return t[:, i:i + 1]


def build_program(debug=False, upto=99):
    nc = bass.Bass("TRN2", target_bir_lowering=False)

    x_hbm = nc.dram_tensor("x_sh", [T, D], F32, kind="ExternalInput").ap()
    maskB_hbm = nc.dram_tensor("maskB", [H, T], F32, kind="ExternalInput").ap()
    Wx_hbm = nc.dram_tensor("Wx", [D, 2 * N + H], F32, kind="ExternalInput").ap()
    Wg_hbm = nc.dram_tensor("Wg", [D, D], F32, kind="ExternalInput").ap()
    bgc_hbm = nc.dram_tensor("bgc", [128, 16], F32, kind="ExternalInput").ap()
    Wo_hbm = nc.dram_tensor("Wo", [D, D], F32, kind="ExternalInput").ap()
    At_hbm = nc.dram_tensor("At", [N, H], F32, kind="ExternalInput").ap()
    Dpc_hbm = nc.dram_tensor("Dpc", [128, 16], F32, kind="ExternalInput").ap()
    w1T_hbm = nc.dram_tensor("w1T", [D, INNER], F32, kind="ExternalInput").ap()
    b1B_hbm = nc.dram_tensor("b1B", [128, 64], F32, kind="ExternalInput").ap()
    w2T_hbm = nc.dram_tensor("w2T", [INNER, D], F32, kind="ExternalInput").ap()
    b2B_hbm = nc.dram_tensor("b2B", [128, D], F32, kind="ExternalInput").ap()
    Est_hbm = nc.dram_tensor("Est", [H, 128, 32], F32, kind="ExternalInput").ap()
    sel_hbm = nc.dram_tensor("selst", [16, 64, 128], BF16, kind="ExternalInput").ap()
    bsel_hbm = nc.dram_tensor("bsel", [H, 32, 128], BF16, kind="ExternalInput").ap()
    eyeb_hbm = nc.dram_tensor("eyeb", [128, 128], BF16, kind="ExternalInput").ap()
    eyef_hbm = nc.dram_tensor("eyef", [128, 128], F32, kind="ExternalInput").ap()
    epsc_hbm = nc.dram_tensor("epsc", [128, 8], F32, kind="ExternalInput").ap()

    out_hbm = nc.dram_tensor("out", [TOWN, D], F32, kind="ExternalOutput").ap()

    s1T_hbm = nc.dram_tensor("s1Tsc", [64, 128, INNER], BF16, kind="Internal").ap()
    hg_hbm = nc.dram_tensor("hgsc", [INNER, TOWN], F32, kind="Internal").ap()
    k2T_hbm = nc.dram_tensor("k2Tsc", [INNER, TOWN], BF16, kind="Internal").ap()
    s_hbm = nc.dram_tensor("ssc", [TOWN, D], F32, kind="Internal").ap()

    dbg = {}
    if debug:
        for name, shape, dt in [("xnT", [D, T], BF16), ("k1T", [D, TOWN], BF16),
                                ("Bfm", [N, T], F32), ("Cfm", [N, T], F32),
                                ("db", [32, T], F32), ("y", [32, TOWN], F32),
                                ("yfull", [D, TOWN], BF16), ("s", [TOWN, D], F32),
                                ("alpha", [1, 4], F32), ("S1", [INNER, TOWN], F32),
                                ("k2T", [INNER, TOWN], BF16), ("gfm", [TOWN, D], F32)]:
            dbg[name] = nc.dram_tensor("d_" + name, shape, dt, kind="ExternalOutput").ap()

    with ExitStack() as octx:
        bld = Builder(nc)
        bld.make_sems(octx)
        sb = lambda nm, shape, dt: octx.enter_context(nc.sbuf_tensor(nm, shape, dt))
        psd = lambda nm, fr: octx.enter_context(nc.psum_tensor(nm, [128, fr], F32))

        cols = sb("cols", [128, 64], F32)
        qsm = sb("qsm", [128, 128], F32)
        consts = sb("consts", [128, 128], BF16)
        eyef = sb("eyef_t", [128, 128], F32)
        At = sb("At_t", [128, 32], F32)
        Dpc = sb("Dpc_t", [128, 16], F32)
        bgc = sb("bgc_t", [128, 16], F32)
        Est = sb("Est_t", [128, H * 32], F32)
        selst = sb("selst_t", [64, 16 * 128], BF16)
        bsel = sb("bsel_t", [32, H * 128], BF16)
        alphas = sb("alphas", [128, 96], F32)
        onesf = sb("onesf", [128, 128], F32)
        ybuf = sb("ybuf", [128, 2 * TOWN], F32)
        yhl = sb("yhl", [128, TOWN], BF16)
        a1col = sb("a1col", [128, 24], F32)
        aw1 = sb("aw1", [128, 128], F32)
        epsc = sb("epsc_t", [128, 8], F32)

        pmA = psd("pmA", 2048)
        pmB = psd("pmB", 512)
        pmC = octx.enter_context(nc.psum_tensor("pmC", [128, 3072], BF16))

        block = octx.enter_context(nc.Block())
        bld.block = block

        env = dict(locals())

        bld.op(SY, lambda e: e.dma_start(out=At[:N, :H], in_=At_hbm), writes=[At], slot='At')
        bld.op(SY, lambda e: e.dma_start(out=Dpc[:], in_=Dpc_hbm), writes=[Dpc], slot='Dpc')
        bld.op(SY, lambda e: e.dma_start(out=bgc[:], in_=bgc_hbm), writes=[bgc], slot='bgc')
        bld.op(SY, lambda e: e.dma_start(out=eyef[:], in_=eyef_hbm), writes=[eyef], slot='eyef')
        bld.op(SY, lambda e: e.dma_start(out=consts[:], in_=eyeb_hbm), writes=[consts], slot='eyeb')
        bld.op(SY, lambda e: e.dma_start(out=Est.rearrange("p (h c) -> p h c", h=H), in_=Est_hbm.rearrange("h p c -> p h c")),
               writes=[Est], slot='Est')
        bld.op(SY, lambda e: e.dma_start(out=selst.rearrange("k (c p) -> k c p", c=16), in_=sel_hbm.rearrange("c k p -> k c p")),
               writes=[selst], slot='selst')
        bld.op(SY, lambda e: e.dma_start(out=bsel.rearrange("k (h p) -> k h p", h=H), in_=bsel_hbm.rearrange("h k p -> k h p")),
               writes=[bsel], slot='bsel')
        bld.op(SY, lambda e: e.dma_start(out=epsc[:], in_=epsc_hbm), writes=[epsc], slot='epsc')
        bld.op(VE, lambda e: e.memset(onesf[:], 1.0), writes=[onesf])

        with ExitStack() as sK:
            k1T = sK.enter_context(nc.sbuf_tensor("k1T", [128, 16 * TOWN], BF16))
            env["k1T"] = k1T
            with ExitStack() as sW:
                xnT_ow = sW.enter_context(nc.sbuf_tensor("xnT_ow", [128, 16 * TOWN], BF16))
                env["xnT_ow"] = xnT_ow
                with ExitStack() as sC:
                    Cfm = sC.enter_context(nc.sbuf_tensor("Cfm", [128, T], F32))
                    eB = sC.enter_context(nc.sbuf_tensor("eB", [128, T], F32))
                    ehl = sC.enter_context(nc.sbuf_tensor("ehl", [128, 2 * T], BF16))
                    env.update(Cfm=Cfm, eB=eB, ehl=ehl)
                    with ExitStack() as sO:
                        xnT_ot = sO.enter_context(
                            nc.sbuf_tensor("xnT_ot", [128, 16 * TOWN], BF16))
                        env["xnT_ot"] = xnT_ot
                        _p12(nc, bld, dbg, debug, env)
                        bld.cut()
                        if upto <= 2:
                            return nc, dbg
                        _p3(nc, bld, dbg, debug, env)
                        bld.cut()
                        if upto <= 3:
                            return nc, dbg
                    with ExitStack() as s45:
                        w1sl = [s45.enter_context(nc.sbuf_tensor(f"p45_w{i}", [128, 2048], BF16))
                                for i in range(4)]
                        s1dmp = s45.enter_context(nc.sbuf_tensor("p45_dmp", [128, 1024], BF16))
                        env.update(w1sl=w1sl, s1dmp=s1dmp)
                        _p4(nc, bld, dbg, debug, env)
                        _p45(nc, bld, dbg, debug, env)
                        bld.cut()
                    if upto <= 4:
                        return nc, dbg
                _p5(nc, bld, dbg, debug, env)
                bld.cut()
                if upto <= 5:
                    return nc, dbg
            _p6(nc, bld, dbg, debug, env)
            bld.cut()
            if upto <= 6:
                return nc, dbg
        _p8(nc, bld, dbg, debug, env)
        bld.cut()
    return nc, dbg


def xnT_reads(L, which):
    t = L["xnT_ot"] if which == "ot" else L["xnT_ow"]
    return [(t, f"{k}_{q}") for k in range(16) for q in range(4)]


def _p12(nc, bld, dbg, debug, L):
    x_hbm, cols, qsm, consts = L["x_hbm"], L["cols"], L["qsm"], L["consts"]
    epsc = L["epsc"]
    xnT_ot, xnT_ow, k1T, pmC = L["xnT_ot"], L["xnT_ow"], L["k1T"], L["pmC"]
    pm = [pmC[:, 0:256], pmC[:, 1024:1280], pmC[:, 2048:2304]]
    with ExitStack() as ctx:
        xb = ctx.enter_context(nc.sbuf_tensor("p2_xb", [128, 2 * D], F32))
        xnb = ctx.enter_context(nc.sbuf_tensor("p2_xnb", [128, 2 * D], BF16))
        k1b = ctx.enter_context(nc.sbuf_tensor("p2_k1b", [128, 2 * D], BF16))
        wk = ctx.enter_context(nc.sbuf_tensor("p2_wk", [128, D], F32))
        intb = ctx.enter_context(nc.sbuf_tensor("p2_int", [128, D], I32))

        for grp in range(8):
            tiles = [grp * 2, grp * 2 + 1]
            g0 = grp * 2
            for j, i in enumerate(tiles):
                bld.op(SY, lambda e, i=i, j=j: e.dma_start(
                    out=xb[:, j * D:(j + 1) * D], in_=x_hbm[i * 128:(i + 1) * 128, :]),
                    writes=[(xb, j)], slot=f'xb{j}')
            srd = []
            for j, i in enumerate(tiles):
                xa = xb[:, j * D:(j + 1) * D]
                bld.op(SC, lambda e, xa=xa, i=i: e.activation(
                    out=wk[:], in_=xa, func=AF.Square, accum_out=cols[:, 16 + i:17 + i]),
                    reads=[(xb, j)], writes=[(cols, f"ss{i}"), (wk, 0)])
                bld.op(VE, lambda e, xa=xa, i=i: e.reduce_sum(
                    out=cols[:, i:i + 1], in_=xa, axis=AX.X),
                    reads=[(xb, j)], writes=[(cols, f"s{i}")])
                srd += [(cols, f"s{i}"), (cols, f"ss{i}")]
                if i >= 8:
                    bld.op(VE, lambda e, xa=xa, i=i: e.reduce_max(
                        out=cols[:, 24 + i:25 + i], in_=xa, axis=AX.X),
                        reads=[(xb, j)], writes=[(cols, f"mx{i}")])
                    bld.op(VE, lambda e, xa=xa, i=i: e.tensor_reduce(
                        out=cols[:, 32 + i:33 + i], in_=xa, axis=AX.X, op=ALU.min),
                        reads=[(xb, j)], writes=[(cols, f"mn{i}")])
                    srd += [(cols, f"mx{i}"), (cols, f"mn{i}")]
            bld.op(VE, lambda e, g0=g0: e.tensor_scalar(
                out=qsm[:, 0:2], in0=cols[:, g0:g0 + 2], scalar1=1.0 / D,
                scalar2=None, op0=ALU.mult), reads=srd, writes=[(qsm, "m")])
            bld.op(SC, lambda e: e.activation(out=qsm[:, 2:4], in_=qsm[:, 0:2], func=AF.Square),
                   reads=[(qsm, "m")], writes=[(qsm, "m2")])
            bld.op(VE, lambda e, g0=g0: e.scalar_tensor_tensor(
                out=qsm[:, 4:6], in0=cols[:, 16 + g0:18 + g0], scalar=1.0 / D,
                in1=qsm[:, 2:4], op0=ALU.mult, op1=ALU.subtract),
                reads=[(qsm, "m2")] + srd, writes=[(qsm, "var")])
            bld.op(SC, lambda e: e.activation(out=qsm[:, 6:8], in_=qsm[:, 4:6],
                                              func=AF.Sqrt, bias=epsc[:, 0:1]),
                   reads=[(qsm, "var"), epsc], writes=[(qsm, "sd")])
            bld.op(VE, lambda e, g0=g0: e.reciprocal(out=qsm[:, 16 + g0:18 + g0], in_=qsm[:, 6:8]),
                   reads=[(qsm, "sd")], writes=[(qsm, f"rstd{grp}")])
            bld.op(VE, lambda e, g0=g0: e.scalar_tensor_tensor(
                out=qsm[:, 32 + g0:34 + g0], in0=qsm[:, 0:2], scalar=-1.0,
                in1=qsm[:, 16 + g0:18 + g0], op0=ALU.mult, op1=ALU.mult),
                reads=[(qsm, "m"), (qsm, f"rstd{grp}")], writes=[(qsm, f"nmr{grp}")])
            if grp >= 4:
                j0 = (grp - 4) * 2
                rstd2 = qsm[:, 16 + g0:18 + g0]
                rsr = [(qsm, f"rstd{grp}"), (qsm, "m"), (qsm, "var")]
                bld.op(VE, lambda e, g0=g0: e.tensor_sub(
                    qsm[:, 48:50], cols[:, 24 + g0:26 + g0], qsm[:, 0:2]), reads=srd + rsr,
                    writes=[(qsm, "t1")])
                bld.op(VE, lambda e, g0=g0: e.tensor_sub(
                    qsm[:, 50:52], qsm[:, 0:2], cols[:, 32 + g0:34 + g0]), reads=srd + rsr,
                    writes=[(qsm, "t2")])
                bld.op(VE, lambda e: e.tensor_max(qsm[:, 48:50], qsm[:, 48:50], qsm[:, 50:52]),
                       reads=[(qsm, "t1"), (qsm, "t2")], writes=[(qsm, "t1")])
                bld.op(VE, lambda e, rstd2=rstd2: e.tensor_mul(qsm[:, 48:50], qsm[:, 48:50], rstd2),
                       reads=[(qsm, "t1")] + rsr, writes=[(qsm, "t1")])
                bld.op(SC, lambda e, rstd2=rstd2: e.activation(
                    out=qsm[:, 52:54], in_=rstd2, func=AF.Square), reads=rsr, writes=[(qsm, "r2")])
                bld.op(VE, lambda e: e.tensor_mul(qsm[:, 52:54], qsm[:, 52:54], qsm[:, 4:6]),
                       reads=[(qsm, "r2"), (qsm, "var")], writes=[(qsm, "r2")])
                bld.op(SC, lambda e: e.activation(out=qsm[:, 52:54], in_=qsm[:, 52:54],
                                                  func=AF.Sqrt, bias=epsc[:, 1:2]),
                       reads=[(qsm, "r2"), epsc], writes=[(qsm, "r2")])
                bld.op(VE, lambda e: e.tensor_scalar(
                    out=qsm[:, 54:56], in0=qsm[:, 52:54], scalar1=SQD, scalar2=None, op0=ALU.mult),
                    reads=[(qsm, "r2")], writes=[(qsm, "iden")])
                bld.op(VE, lambda e: e.reciprocal(out=qsm[:, 54:56], in_=qsm[:, 54:56]),
                       reads=[(qsm, "iden")], writes=[(qsm, "iden")])
                bld.op(VE, lambda e: e.tensor_mul(qsm[:, 48:50], qsm[:, 48:50], qsm[:, 54:56]),
                       reads=[(qsm, "t1"), (qsm, "iden")], writes=[(qsm, "t1")])
                bld.op(VE, lambda e: e.tensor_scalar(
                    out=qsm[:, 48:50], in0=qsm[:, 48:50], scalar1=1e-5, scalar2=None, op0=ALU.max),
                    reads=[(qsm, "t1")], writes=[(qsm, "t1")])
                bld.op(VE, lambda e, j0=j0: e.tensor_copy(qsm[:, 64 + j0:66 + j0], qsm[:, 48:50]),
                       reads=[(qsm, "t1")], writes=[(qsm, f"mm1_{j0}")])
                bld.op(VE, lambda e: e.reciprocal(out=qsm[:, 50:52], in_=qsm[:, 48:50]),
                       reads=[(qsm, "t1")], writes=[(qsm, "t2")])
                bld.op(VE, lambda e: e.tensor_scalar(
                    out=qsm[:, 50:52], in0=qsm[:, 50:52], scalar1=127.0, scalar2=None, op0=ALU.mult),
                    reads=[(qsm, "t2")], writes=[(qsm, "t2")])
                bld.op(VE, lambda e: e.tensor_mul(qsm[:, 50:52], qsm[:, 50:52], qsm[:, 54:56]),
                       reads=[(qsm, "t2"), (qsm, "iden")], writes=[(qsm, "t2")])
                bld.op(VE, lambda e, rstd2=rstd2, j0=j0: e.tensor_mul(
                    qsm[:, 72 + j0:74 + j0], qsm[:, 50:52], rstd2),
                    reads=[(qsm, "t2")] + rsr, writes=[(qsm, f"c1_{j0}")])
                bld.op(VE, lambda e, j0=j0: e.tensor_mul(
                    qsm[:, 56:58], qsm[:, 0:2], qsm[:, 72 + j0:74 + j0]),
                    reads=[(qsm, "m"), (qsm, f"c1_{j0}")], writes=[(qsm, "mc1")])
                bld.op(VE, lambda e, j0=j0: e.tensor_scalar(
                    out=qsm[:, 80 + j0:82 + j0], in0=qsm[:, 56:58], scalar1=-1.0,
                    scalar2=256.50002, op0=ALU.mult, op1=ALU.add),
                    reads=[(qsm, "mc1")], writes=[(qsm, f"bsh_{j0}")])
                bld.op(VE, lambda e, j0=j0: e.tensor_scalar(
                    out=qsm[:, 88 + j0:90 + j0], in0=qsm[:, 56:58], scalar1=-256.5,
                    scalar2=None, op0=ALU.add),
                    reads=[(qsm, "mc1")], writes=[(qsm, f"thr_{j0}")])
                bld.op(VE, lambda e, j0=j0: e.tensor_scalar(
                    out=qsm[:, 96 + j0:98 + j0], in0=qsm[:, 56:58], scalar1=-255.5,
                    scalar2=None, op0=ALU.add),
                    reads=[(qsm, "mc1")], writes=[(qsm, f"thrB_{j0}")])

            for j, i in enumerate(tiles):
                xa = xb[:, j * D:(j + 1) * D]
                bld.op(VE, lambda e, xa=xa, j=j, g0=g0: e.tensor_scalar(
                    out=xnb[:, j * D:(j + 1) * D], in0=xa,
                    scalar1=qsm[:, 16 + g0 + j:17 + g0 + j],
                    scalar2=qsm[:, 32 + g0 + j:33 + g0 + j],
                    op0=ALU.mult, op1=ALU.add),
                    reads=[(xb, j), (qsm, f"rstd{grp}"), (qsm, f"nmr{grp}")],
                    writes=[(xnb, j)])
            dst = xnT_ot if grp < 4 else xnT_ow
            doff = (grp % 4) * 256
            for k in range(16):
                pb = pm[k % 2]
                for j in range(2):
                    bld.op(TE, lambda e, k=k, j=j, pb=pb: e.transpose(
                        pb[:, j * 128:(j + 1) * 128],
                        xnb[:, j * D + k * 128:j * D + (k + 1) * 128], consts[:]),
                        reads=[(xnb, j), consts], writes=[(pmC, f"C{k % 2}")])
                eng = SC if k % 2 == 0 else VE
                def evac(e, pb=pb, k=k, dst=dst, doff=doff, eng=eng):
                    dsl = dst[:, k * TOWN + doff:k * TOWN + doff + 256]
                    if eng == SC:
                        return e.activation(out=dsl, in_=pb[:, 0:256], func=AF.Copy)
                    return e.tensor_copy(dsl, pb[:, 0:256])
                bld.op(eng, evac, reads=[(pmC, f"C{k % 2}")],
                       writes=[(dst, f"{k}_{grp % 4}")])
            if grp >= 4:
                j0 = (grp - 4) * 2
                for j, i in enumerate(tiles):
                    xa = xb[:, j * D:(j + 1) * D]
                    c1 = qsm[:, 72 + j0 + j:73 + j0 + j]
                    bsh = qsm[:, 80 + j0 + j:81 + j0 + j]
                    thr = qsm[:, 88 + j0 + j:89 + j0 + j]
                    thrB = qsm[:, 96 + j0 + j:97 + j0 + j]
                    crd = [(qsm, f"c1_{j0}"), (qsm, f"bsh_{j0}"), (qsm, f"thr_{j0}"),
                           (qsm, f"thrB_{j0}")]
                    bld.op(VE, lambda e, xa=xa, c1=c1, bsh=bsh: e.tensor_scalar(
                        out=intb[:], in0=xa, scalar1=c1, scalar2=bsh,
                        op0=ALU.mult, op1=ALU.add),
                        reads=[(xb, j)] + crd, writes=[(intb, 0)])
                    bld.op(VE, lambda e, xa=xa, c1=c1: e.scalar_tensor_tensor(
                        out=wk[:], in0=xa, scalar=c1, in1=intb[:],
                        op0=ALU.mult, op1=ALU.subtract),
                        reads=[(xb, j), (intb, 0)] + crd, writes=[(wk, 0)])
                    bld.op(VE, lambda e, thr=thr: e.tensor_scalar(
                        out=xnb[:, 0:D], in0=wk[:], scalar1=thr, scalar2=None, op0=ALU.is_gt),
                        reads=[(wk, 0)] + crd, writes=[(xnb, 0)])
                    bld.op(VE, lambda e, thrB=thrB: e.scalar_tensor_tensor(
                        out=wk[:], in0=wk[:], scalar=thrB, in1=xnb[:, 0:D],
                        op0=ALU.is_gt, op1=ALU.add),
                        reads=[(wk, 0), (xnb, 0)] + crd, writes=[(wk, 0)])
                    bld.op(VE, lambda e, j=j: e.scalar_tensor_tensor(
                        out=k1b[:, j * D:(j + 1) * D], in0=wk[:], scalar=257.0, in1=intb[:],
                        op0=ALU.subtract, op1=ALU.add),
                        reads=[(wk, 0), (intb, 0)], writes=[(k1b, j)])
                for k in range(16):
                    pb = pm[2]
                    for j in range(2):
                        bld.op(TE, lambda e, k=k, j=j, pb=pb: e.transpose(
                            pb[:, j * 128:(j + 1) * 128],
                            k1b[:, j * D + k * 128:j * D + (k + 1) * 128], consts[:]),
                            reads=[(k1b, j), consts], writes=[(pmC, "C2")])
                    bld.op(VE, lambda e, pb=pb, k=k, j0=j0: e.tensor_copy(
                        k1T[:, k * TOWN + j0 * 128:k * TOWN + (j0 + 2) * 128], pb[:, 0:256]),
                        reads=[(pmC, "C2")],
                        writes=[(k1T, f"{k}_{j0}")])
        if debug:
            for k in range(16):
                bld.op(SY, lambda e, k=k: e.dma_start(
                    out=dbg["xnT"][k * 128:(k + 1) * 128, 0:TOWN],
                    in_=xnT_ot[:, k * TOWN:(k + 1) * TOWN]),
                    reads=[(xnT_ot, f"{k}_{q}") for q in range(4)], slot='dbg1')
                bld.op(SY, lambda e, k=k: e.dma_start(
                    out=dbg["xnT"][k * 128:(k + 1) * 128, TOWN:T],
                    in_=xnT_ow[:, k * TOWN:(k + 1) * TOWN]),
                    reads=[(xnT_ow, f"{k}_{q}") for q in range(4)], slot='dbg2')
                bld.op(SY, lambda e, k=k: e.dma_start(
                    out=dbg["k1T"][k * 128:(k + 1) * 128, :],
                    in_=k1T[:, k * TOWN:(k + 1) * TOWN]),
                    reads=[(k1T, f"{k}_{j0}") for j0 in (0, 2, 4, 6)], slot='dbg3')


def _p3(nc, bld, dbg, debug, L):
    xnT_ot, xnT_ow, Cfm, eB, ehl = L["xnT_ot"], L["xnT_ow"], L["Cfm"], L["eB"], L["ehl"]
    pmA, pmB = L["pmA"], L["pmB"]
    Wx_hbm, maskB_hbm = L["Wx_hbm"], L["maskB_hbm"]
    pm = [pmA[:, i * 512:(i + 1) * 512] for i in range(4)]
    pdb = pmB[0:32, 0:512]
    with ExitStack() as ctx:
        WxB = ctx.enter_context(nc.sbuf_tensor("WxB", [128, 16 * 288], BF16))
        Bfm = ctx.enter_context(nc.sbuf_tensor("Bfm", [128, T], F32))
        dwk = ctx.enter_context(nc.sbuf_tensor("dwk", [128, 2 * T], F32))
        bld.op(GP, lambda e: e.dma_start(
            out=WxB.rearrange("p (k c) -> p k c", k=16), in_=Wx_hbm.rearrange("(k p) c -> p k c", p=128)),
            writes=[WxB], slot='WxB')
        bld.op(SY, lambda e: e.dma_start(out=dwk[:H, T:2 * T], in_=maskB_hbm),
               writes=[(dwk, "R")], slot='mask')
        for half, which in ((0, "ot"), (1, "ow")):
            src = xnT_ot if which == "ot" else xnT_ow
            rds = xnT_reads(L, which)
            for tc in range(2):
                c0 = tc * 512
                for (dst, co, tag, pb, pt) in ((Bfm, 0, "B", pm[tc % 2], f"A{tc % 2}"),
                                               (Cfm, 128, "C", pm[2 + tc % 2], f"A{2 + tc % 2}")):
                    for k in range(16):
                        bld.op(TE, lambda e, k=k, pb=pb, co=co, src=src, c0=c0: e.matmul(
                            pb[:, 0:512], WxB[:, k * 288 + co:k * 288 + co + 128],
                            src[:, k * TOWN + c0:k * TOWN + c0 + 512],
                            start=(k == 0), stop=(k == 15)),
                            reads=[WxB] + rds, writes=[(pmA, pt)], inc=(1 if k == 15 else 0))
                    bld.op(VE, lambda e, pb=pb, dst=dst, half=half, c0=c0: e.tensor_copy(
                        dst[:, half * TOWN + c0:half * TOWN + c0 + 512], pb[:, 0:512]),
                        reads=[(pmA, pt)], writes=[(dst, f"{tag}{half}{c0}")])
                for k in range(16):
                    bld.op(TE, lambda e, k=k, src=src, c0=c0: e.matmul(
                        pdb, WxB[:, k * 288 + 256:k * 288 + 288],
                        src[:, k * TOWN + c0:k * TOWN + c0 + 512],
                        start=(k == 0), stop=(k == 15)),
                        reads=[WxB] + rds, writes=[(pmB, "B0")], inc=(1 if k == 15 else 0))
                bld.op(VE, lambda e, half=half, c0=c0: e.tensor_copy(
                    dwk[:32, half * TOWN + c0:half * TOWN + c0 + 512], pdb),
                    reads=[(pmB, "B0")], writes=[(dwk, f"L{half}{c0}")])
        dbrd = [(dwk, f"L{h}{t}") for h in range(2) for t in (0, 512)]
        if debug:
            bld.op(SY, lambda e: e.dma_start(out=dbg["db"], in_=dwk[:32, 0:T]), reads=dbrd, slot="dbg4")
        bld.op(VE, lambda e: e.tensor_add(dwk[:32, T:2 * T], dwk[:32, 0:T], dwk[:32, T:2 * T]),
               reads=dbrd + [(dwk, "R")], writes=[(dwk, "R")])
        bld.op(SC, lambda e: e.activation(out=dwk[:32, T:2 * T], in_=dwk[:32, T:2 * T], func=AF.Exp),
               reads=[(dwk, "R")], writes=[(dwk, "R")])
        bld.op(VE, lambda e: e.tensor_copy(ehl[:32, 0:T], dwk[:32, T:2 * T]),
               reads=[(dwk, "R")], writes=[(ehl, "hi")])
        bld.op(VE, lambda e: e.tensor_sub(ehl[:32, T:2 * T], dwk[:32, T:2 * T], ehl[:32, 0:T]),
               reads=[(dwk, "R"), (ehl, "hi")], writes=[(ehl, "lo")])
        Brd = [(Bfm, f"B{h}{t}") for h in range(2) for t in (0, 512)]
        bld.op(SC, lambda e: e.activation(out=eB[:], in_=Bfm[:], func=AF.Exp),
               reads=Brd, writes=[eB])
        if debug:
            bld.op(SY, lambda e: e.dma_start(out=dbg["Bfm"], in_=Bfm[:]), reads=Brd, slot="dbg5")
            bld.op(SY, lambda e: e.dma_start(out=dbg["Cfm"], in_=Cfm[:]),
                   reads=[(Cfm, f"C{h}{t}") for h in range(2) for t in (0, 512)], slot="dbg6")


def _p4(nc, bld, dbg, debug, L):
    Cfm, eB, ehl, Est, At = L["Cfm"], L["eB"], L["ehl"], L["Est"], L["At"]
    bsel, pmA, pmB, ybuf, yhl = L["bsel"], L["pmA"], L["pmB"], L["ybuf"], L["yhl"]
    alphas, onesf, a1col, qsm = L["alphas"], L["onesf"], L["a1col"], L["qsm"]
    w1T_hbm, s1T_hbm = L["w1T_hbm"], L["s1T_hbm"]
    w1sl, s1dmp = L["w1sl"], L["s1dmp"]
    pwH = pmA[:, 0:1024]
    pyv = [pmA[:, 1024:1536], pmA[:, 1536:2048]]
    Crd = [(Cfm, f"C{h}{t}") for h in range(2) for t in (0, 512)]
    with ExitStack() as ctx:
        d_t = [ctx.enter_context(nc.sbuf_tensor(f"p4_d{i}", [128, T], F32)) for i in range(3)]
        a_t = [ctx.enter_context(nc.sbuf_tensor(f"p4_a{i}", [128, T], F32)) for i in range(3)]
        h_t = [ctx.enter_context(nc.sbuf_tensor(f"p4_h{i}", [128, T], F32)) for i in range(2)]
        yp_t = [ctx.enter_context(nc.sbuf_tensor(f"p4_yp{i}", [128, TOWN], F32))
                for i in range(2)]

        def tail_vg(h):
            at_, ut_, ht_, ypt = a_t[h % 3], d_t[h % 3], h_t[h % 2], yp_t[h % 2]
            bld.op(VE, lambda e, at_=at_, ut_=ut_, ht_=ht_: e.tensor_tensor_scan(
                out=ht_[:], data0=at_[:], data1=ut_[:],
                initial=0.0, op0=ALU.mult, op1=ALU.add),
                   reads=[at_, (ut_, 0), (ut_, 1)], writes=[ht_])
            bld.op(GP, lambda e, ht_=ht_, ypt=ypt: e.tensor_mul(
                ypt[:], ht_[:, TOWN:T], Cfm[:, TOWN:T]),
                   reads=[ht_] + Crd, writes=[ypt])

        def tail_y(h):
            ypt = yp_t[h % 2]
            for tc in range(2):
                bld.op(TE, lambda e, h=h, tc=tc, ypt=ypt: e.matmul(
                    pyv[tc][0:32, :], Est[:, h * 32:(h + 1) * 32],
                    ypt[:, tc * 512:(tc + 1) * 512],
                    start=(h == 0), stop=(h == H - 1)),
                    reads=[Est, ypt], writes=[(pmA, f"A{2 + tc}")])

        def w1_load(q):
            kq, ih = q // 4, q % 4
            wb = w1sl[q % 4]
            bld.op(GP, lambda e, kq=kq, ih=ih, wb=wb: e.dma_start(
                out=wb[:],
                in_=w1T_hbm[kq * 128:(kq + 1) * 128, ih * 2048:(ih + 1) * 2048]),
                writes=[(wb, 0)], slot=f'w1e{q % 4}')

        bld.op(VE, lambda e: e.memset(L["aw1"][:, 64:128], 0.0),
               writes=[(L["aw1"], "zz")])
        for q in range(2):
            w1_load(q)
        for h in range(H):
            dt_, at_ = d_t[h % 3], a_t[h % 3]
            for half in (0, 1):
                hoff = half * TOWN
                for tc in (0, 1):
                    for part in (0, 1):
                        bld.op(TE, lambda e, h=h, tc=tc, part=part, hoff=hoff: e.matmul(
                            pwH[:, tc * 512:(tc + 1) * 512],
                            bsel[:, h * 128:(h + 1) * 128],
                            ehl[:32, part * T + hoff + tc * 512:part * T + hoff + (tc + 1) * 512],
                            start=(part == 0), stop=(part == 1)),
                            reads=[bsel, (ehl, "hi"), (ehl, "lo")], writes=[(pmA, f"A{tc}")],
                            inc=(1 if part == 1 else 0))
                bld.op(VE, lambda e, hoff=hoff, dt_=dt_: e.tensor_mul(
                    dt_[:, hoff:hoff + TOWN], eB[:, hoff:hoff + TOWN], pwH[:, :]),
                    reads=[eB, (pmA, "A0"), (pmA, "A1")], writes=[(dt_, half)])
            if h > 1:
                tail_vg(h - 2)
            if h > 2:
                tail_y(h - 3)
            for q in (2 * h + 2, 2 * h + 3):
                if q < 64:
                    w1_load(q)
            for q in (2 * h, 2 * h + 1):
                kq, ih = q // 4, q % 4
                wb = w1sl[q % 4]
                aw1_ = L["aw1"]
                if q % 2:
                    for hf in range(2):
                        bld.op(SC, lambda e, wb=wb, q=q, hf=hf, aw1_=aw1_: e.activation(
                            out=s1dmp[:], in_=wb[:, hf * 1024:(hf + 1) * 1024], func=AF.Abs,
                            accum_out=aw1_[:, 64 * hf + q:64 * hf + q + 1]),
                            reads=[(wb, 0)], writes=[s1dmp, (aw1_, f"a1_{64 * hf + q}")])
                else:
                    bld.op(VE, lambda e, wb=wb, q=q, aw1_=aw1_: e.reduce_sum(
                        out=aw1_[:, q:q + 1], in_=wb[:], axis=AX.X,
                        apply_absolute_value=True),
                        reads=[(wb, 0)], writes=[(aw1_, f"a1_{q}")])
                bld.op(SC, lambda e, wb=wb: e.activation(
                    out=wb[:], in_=wb[:], func=AF.Sign),
                    reads=[(wb, 0)], writes=[(wb, 0)])
                for j in range(4):
                    bld.op(SY, lambda e, kq=kq, ih=ih, j=j, wb=wb: e.dma_start(
                        out=s1T_hbm[4 * ih + j, :, kq * 512:(kq + 1) * 512],
                        in_=wb[:, j * 512:(j + 1) * 512]),
                        reads=[(wb, 0)],
                        writes=[(s1T_hbm, f"{4 * ih + j}_{kq}")], slot=f's1Tw{(q % 4) * 4 + j}')
            bld.op(SC, lambda e, dt_=dt_: e.activation(out=dt_[:], in_=dt_[:], func=AF.Ln, bias=1.0),
                   reads=[(dt_, 0), (dt_, 1)], writes=[(dt_, 0), (dt_, 1)])
            bld.op(SC, lambda e, h=h, dt_=dt_, at_=at_: e.activation(
                out=at_[:], in_=dt_[:], func=AF.Exp, scale=At[:, h:h + 1]),
                   reads=[(dt_, 0), (dt_, 1), At], writes=[at_])
            bld.op(GP, lambda e, dt_=dt_: e.tensor_mul(
                dt_[:], dt_[:], dt_[:]),
                   reads=[(dt_, 0), (dt_, 1)], writes=[(dt_, 0), (dt_, 1)])
        tail_vg(H - 2)
        tail_y(H - 3)
        tail_vg(H - 1)
        tail_y(H - 2)
        tail_y(H - 1)
        bld.op(VE, lambda e: e.tensor_copy(ybuf[:32, 0:512], pyv[0][0:32, :]),
               reads=[(pmA, "A2")], writes=[(ybuf, "y")])
        bld.op(VE, lambda e: e.tensor_copy(ybuf[:32, 512:1024], pyv[1][0:32, :]),
               reads=[(pmA, "A3")], writes=[(ybuf, "y")])
        bld.op(VE, lambda e: e.tensor_copy(yhl[:32, :], ybuf[:32, 0:TOWN]),
               reads=[(ybuf, "y")], writes=[(yhl, "hi")])
        bld.op(VE, lambda e: e.tensor_sub(ybuf[:32, TOWN:2 * TOWN], ybuf[:32, 0:TOWN], yhl[:32, :]),
               reads=[(ybuf, "y"), (yhl, "hi")], writes=[(ybuf, "lof")])
        bld.op(GP, lambda e: e.dma_start(out=yhl[32:64, :], in_=ybuf[:32, TOWN:2 * TOWN]),
               reads=[(ybuf, "lof")], writes=[(yhl, "lo")], slot='ylo')
        if debug:
            bld.op(SY, lambda e: e.dma_start(out=dbg["y"], in_=ybuf[:32, 0:TOWN]),
                   reads=[(ybuf, "y")], slot='dbg7')


def _p45(nc, bld, dbg, debug, L):
    alphas, onesf, a1col, qsm = L["alphas"], L["onesf"], L["a1col"], L["qsm"]
    aw1 = L["aw1"]
    pmB = L["pmB"]
    ard = ([(aw1, f"a1_{oc}") for oc in range(64)]
           + [(aw1, f"a1_{64 + oc}") for oc in range(1, 64, 2)] + [(aw1, "zz")])
    bld.op(VE, lambda e: e.reduce_sum(out=alphas[:, 80:81], in_=aw1[:, 0:128], axis=AX.X),
           reads=ard, writes=[(alphas, "rs1")])
    bld.op(TE, lambda e: e.matmul(pmB[0:1, 0:1], onesf[:, 0:1], alphas[:, 80:81],
                                  start=True, stop=True),
           reads=[(alphas, "rs1"), onesf], writes=[(pmB, "B0")])
    bld.op(VE, lambda e: e.tensor_scalar(
        out=alphas[0:1, 82:83], in0=pmB[0:1, 0:1], scalar1=1.0 / (INNER * D),
        scalar2=None, op0=ALU.mult),
        reads=[(pmB, "B0")], writes=[(alphas, "al1s")])
    bld.op(TE, lambda e: e.matmul(pmB[:, 4:5], onesf[0:1, 0:128], alphas[0:1, 82:83],
                                  start=True, stop=True),
           reads=[(alphas, "al1s"), onesf], writes=[(pmB, "B0")])
    bld.op(VE, lambda e: e.tensor_scalar(
        out=a1col[:, 0:8], in0=qsm[:, 64:72], scalar1=pmB[:, 4:5], scalar2=1.0 / 127.0,
        op0=ALU.mult, op1=ALU.mult),
        reads=[(pmB, "B0")] + [(qsm, f"mm1_{j0}") for j0 in (0, 2, 4, 6)],
        writes=[(a1col, "a1")])
    if debug:
        bld.op(SY, lambda e: e.dma_start(out=dbg["alpha"][0:1, 0:1], in_=alphas[0:1, 82:83]),
               reads=[(alphas, "al1s")], slot='dbgA')


def _p5(nc, bld, dbg, debug, L):
    xnT_ow, yhl, selst, Dpc, bgc = L["xnT_ow"], L["yhl"], L["selst"], L["Dpc"], L["bgc"]
    pmA, pmB, eyef = L["pmA"], L["pmB"], L["eyef"]
    Wg_hbm, Wo_hbm, s_hbm = L["Wg_hbm"], L["Wo_hbm"], L["s_hbm"]
    pg = pmA[:, 0:1024]
    pyb = pmA[:, 1024:2048]
    ps1 = pmA[:, 0:1024]
    ptr = [pmA[:, 1024:1152], pmA[:, 1536:1664]]
    xrd = xnT_reads(L, "ow")
    with ExitStack() as ctx:
        WB = ctx.enter_context(nc.sbuf_tensor("p5_WB", [128, 16 * D], BF16))
        yfB = ctx.enter_context(nc.sbuf_tensor("p5_yfB", [128, 16 * TOWN], BF16))
        gt = ctx.enter_context(nc.sbuf_tensor("p5_gt", [128, TOWN], F32))
        sf = ctx.enter_context(nc.sbuf_tensor("p5_sf", [128, TOWN], F32))
        sx = ctx.enter_context(nc.sbuf_tensor("p5_sx", [128, TOWN], F32))
        bld.op(GP, lambda e: e.dma_start(
            out=WB.rearrange("p (k c) -> p k c", k=16), in_=Wg_hbm.rearrange("(k p) c -> p k c", p=128)),
            writes=[(WB, 0)], slot='WB')
        for c in range(16):
            for k in range(16):
                for tc in range(2):
                    bld.op(TE, lambda e, c=c, k=k, tc=tc: e.matmul(
                        pg[:, tc * 512:(tc + 1) * 512],
                        WB[:, k * D + c * 128:k * D + (c + 1) * 128],
                        xnT_ow[:, k * TOWN + tc * 512:k * TOWN + (tc + 1) * 512],
                        start=(k == 0), stop=(k == 15)),
                        reads=[(WB, 0)] + xrd, writes=[(pmA, "A0"), (pmA, "A1")],
                        inc=(1 if (k == 15 and tc == 1) else 0))
            bld.op(SC, lambda e, c=c: e.activation(
                out=gt[:], in_=pg[:, :], func=AF.Sigmoid, bias=bgc[:, c:c + 1]),
                reads=[(pmA, "A0"), (pmA, "A1")], writes=[gt])
            for tc in range(2):
                bld.op(TE, lambda e, c=c, tc=tc: e.matmul(
                    pyb[:, tc * 512:(tc + 1) * 512], selst[:, c * 128:(c + 1) * 128],
                    yhl[0:64, tc * 512:(tc + 1) * 512], start=True, stop=True),
                    reads=[selst, (yhl, "hi"), (yhl, "lo")], writes=[(pmA, f"A{2 + tc}")])
            bld.op(VE, lambda e, c=c: e.tensor_scalar(
                out=sf[:], in0=xnT_ow[:, c * TOWN:(c + 1) * TOWN],
                scalar1=Dpc[:, c:c + 1], scalar2=None, op0=ALU.mult),
                reads=xrd + [Dpc], writes=[(sf, 0)])
            bld.op(VE, lambda e: e.tensor_add(sf[:], sf[:], pyb[:, :]),
                   reads=[(sf, 0), (pmA, "A2"), (pmA, "A3")], writes=[(sf, 0)])
            bld.op(VE, lambda e, c=c: e.tensor_mul(
                yfB[:, c * TOWN:(c + 1) * TOWN], sf[:], gt[:]),
                reads=[(sf, 0), gt], writes=[(yfB, c)])
        if debug:
            for k in range(16):
                bld.op(SY, lambda e, k=k: e.dma_start(
                    out=dbg["yfull"][k * 128:(k + 1) * 128, :],
                    in_=yfB[:, k * TOWN:(k + 1) * TOWN]), reads=[(yfB, k)], slot='dbg8')
        bld.op(GP, lambda e: e.dma_start(
            out=WB.rearrange("p (k c) -> p k c", k=16), in_=Wo_hbm.rearrange("(k p) c -> p k c", p=128)),
            writes=[(WB, 0)], slot='WB')
        yfrd = [(yfB, c) for c in range(16)]
        for c in range(16):
            for k in range(16):
                for tc in range(2):
                    bld.op(TE, lambda e, c=c, k=k, tc=tc: e.matmul(
                        ps1[:, tc * 512:(tc + 1) * 512],
                        WB[:, k * D + c * 128:k * D + (c + 1) * 128],
                        yfB[:, k * TOWN + tc * 512:k * TOWN + (tc + 1) * 512],
                        start=(k == 0), stop=(k == 15)),
                        reads=[(WB, 0)] + yfrd, writes=[(pmA, "A0"), (pmA, "A1")],
                        inc=(1 if (k == 15 and tc == 1) else 0))
            bld.op(VE, lambda e: e.tensor_copy(sf[:], ps1[:, :]),
                   reads=[(pmA, "A0"), (pmA, "A1")], writes=[(sf, 0)])
            for j in range(8):
                pb = ptr[j % 2]
                bld.op(TE, lambda e, j=j, pb=pb: e.transpose(
                    pb, sf[:, j * 128:(j + 1) * 128], eyef[:]),
                    reads=[(sf, 0), eyef], writes=[(pmA, f"A{2 + j % 2}")])
                bld.op(VE, lambda e, j=j, pb=pb: e.tensor_copy(
                    sx[:, j * 128:(j + 1) * 128], pb),
                    reads=[(pmA, f"A{2 + j % 2}")], writes=[(sx, j)])
            bld.op(SY, lambda e, c=c: e.dma_start(
                out=s_hbm[0:TOWN, c * 128:(c + 1) * 128].rearrange("(j p) c -> p j c", p=128),
                in_=sx.rearrange("p (j c) -> p j c", c=128)),
                reads=[(sx, j) for j in range(8)], writes=[(s_hbm, c)], slot='swr')
            if debug:
                bld.op(SY, lambda e, c=c: e.dma_start(
                    out=dbg["s"][0:TOWN, c * 128:(c + 1) * 128].rearrange("(j p) c -> p j c", p=128),
                    in_=sx.rearrange("p (j c) -> p j c", c=128)),
                    reads=[(sx, j) for j in range(8)], slot='dbg9')


def _p6(nc, bld, dbg, debug, L):
    k1T, alphas, a1col, epsc = L["k1T"], L["alphas"], L["a1col"], L["epsc"]
    pmA, pmB = L["pmA"], L["pmB"]
    onesf, eyef = L["onesf"], L["eyef"]
    s1T_hbm, hg_hbm, k2T_hbm, b1c_hbm = (
        L["s1T_hbm"], L["hg_hbm"], L["k2T_hbm"], L["b1B_hbm"])
    pm = [pmA[:, i * 512:(i + 1) * 512] for i in range(3)]
    k1rd = [(k1T, f"{k}_{j0}") for k in range(16) for j0 in (0, 2, 4, 6)]
    with ExitStack() as ctx:
        b1c = ctx.enter_context(nc.sbuf_tensor("p6_b1c", [128, 64], F32))
        acs = ctx.enter_context(nc.sbuf_tensor("p6_acs", [128, TOWN], F32))
        acm = ctx.enter_context(nc.sbuf_tensor("p6_acm", [128, TOWN], F32))
        rows = ctx.enter_context(nc.sbuf_tensor("p6_rows", [128, 8 * TOWN], F32))
        a1b = ctx.enter_context(nc.sbuf_tensor("p6_a1b", [128, TOWN], F32))
        c2b = ctx.enter_context(nc.sbuf_tensor("p6_c2b", [128, TOWN], F32))
        hgc2 = [ctx.enter_context(nc.sbuf_tensor(f"p6_hg{i}", [128, TOWN], F32))
                for i in range(2)]
        bld.op(SY, lambda e: e.dma_start(out=b1c[:, 0:64], in_=b1c_hbm),
               writes=[b1c], slot='b1B')
        for tt in range(8):
            bk = "A0" if tt < 4 else "A1"
            bld.op(TE, lambda e, tt=tt: e.matmul(
                pmA[0:1, tt * 128:(tt + 1) * 128], a1col[:, tt:tt + 1],
                eyef[:, 0:128], start=True, stop=True),
                reads=[(a1col, "a1"), eyef], writes=[(pmA, bk)])
        bld.op(VE, lambda e: e.tensor_copy(rows[0:1, 0 * TOWN:1 * TOWN], pmA[0:1, 0:TOWN]),
               reads=[(pmA, "A0"), (pmA, "A1")], writes=[(rows, "a1r")])
        for tc in range(2):
            bk = f"A{tc}"
            bld.op(TE, lambda e, tc=tc: e.matmul(
                pm[tc][:, 0:512], onesf[0:1, 0:128],
                rows[0:1, 0 * TOWN + tc * 512:0 * TOWN + (tc + 1) * 512], start=True, stop=True),
                reads=[(rows, "a1r"), onesf], writes=[(pmA, bk)])
            bld.op(VE, lambda e, tc=tc: e.tensor_copy(
                a1b[:, tc * 512:(tc + 1) * 512], pm[tc][:, 0:512]),
                reads=[(pmA, bk)], writes=[(a1b, 0)])
        with ExitStack() as cS:
            ssl2 = [cS.enter_context(nc.sbuf_tensor(f"p6_s{i}", [128, 16 * 512], BF16))
                    for i in range(2)]
            h1b2 = [cS.enter_context(nc.sbuf_tensor(f"p6_h1{i}", [128, 512], F32))
                    for i in range(2)]
            h2b2 = [cS.enter_context(nc.sbuf_tensor(f"p6_h2{i}", [128, TOWN], F32))
                    for i in range(2)]
            for oc in range(16):
                sb_ = ssl2[oc % 2]
                bld.op(SY, lambda e, oc=oc, sb_=sb_: e.dma_start(
                    out=sb_[:], in_=s1T_hbm[oc]),
                    reads=[(L["s1T_hbm"], f"{oc}_{kq}") for kq in range(16)],
                    writes=[(sb_, 0)], slot=f'w1s{oc % 2}')
                for icl in range(4):
                    ic = oc * 4 + icl
                    hgc = hgc2[ic % 2]
                    h2b = h2b2[ic % 2]
                    for tc in range(2):
                        bi = (2 * ic + tc) % 3
                        pb = pm[bi]
                        for k in range(16):
                            bld.op(TE, lambda e, k=k, icl=icl, tc=tc, pb=pb, sb_=sb_: e.matmul(
                                pb[:, 0:512],
                                sb_[:, k * 512 + icl * 128:k * 512 + (icl + 1) * 128],
                                k1T[:, k * TOWN + tc * 512:k * TOWN + (tc + 1) * 512],
                                start=(k == 0), stop=(k == 15)),
                                reads=[(sb_, 0)] + k1rd, writes=[(pmA, f"A{bi}")],
                                inc=(1 if k == 15 else 0))
                        h1 = h1b2[tc]
                        bld.op(VE, lambda e, pb=pb, h1=h1, tc=tc: e.tensor_mul(
                            h1[:], pb[:, 0:512], a1b[:, tc * 512:(tc + 1) * 512]),
                            reads=[(pmA, f"A{bi}"), (a1b, 0)], writes=[(h1b2[tc], 0)])
                        bld.op(SC, lambda e, h1=h1, hgc=hgc, tc=tc, ic=ic: e.activation(
                            out=hgc[:, tc * 512:(tc + 1) * 512], in_=h1[:], func=AF.Gelu,
                            bias=b1c[:, ic:ic + 1]),
                            reads=[(h1b2[tc], 0), b1c], writes=[(hgc, tc)])
                        bld.op(SC, lambda e, hgc=hgc, h2b=h2b, tc=tc: e.activation(
                            out=h2b[:, tc * 512:(tc + 1) * 512],
                            in_=hgc[:, tc * 512:(tc + 1) * 512], func=AF.Square),
                            reads=[(hgc, tc)], writes=[(h2b, tc)])
                    if ic == 0:
                        bld.op(VE, lambda e, h2b=h2b: e.tensor_copy(acs[:], h2b[:]),
                               reads=[(h2b, 0), (h2b, 1)], writes=[(acs, 0)])
                        bld.op(VE, lambda e, h2b=h2b: e.tensor_copy(acm[:], h2b[:]),
                               reads=[(h2b, 0), (h2b, 1)], writes=[(acm, 0)])
                    else:
                        bld.op(VE, lambda e, h2b=h2b: e.tensor_add(acs[:], acs[:], h2b[:]),
                               reads=[(h2b, 0), (h2b, 1), (acs, 0)], writes=[(acs, 0)])
                        bld.op(VE, lambda e, h2b=h2b: e.tensor_max(acm[:], acm[:], h2b[:]),
                               reads=[(h2b, 0), (h2b, 1), (acm, 0)], writes=[(acm, 0)])
                    bld.op(SY, lambda e, hgc=hgc, ic=ic: e.dma_start(
                        out=hg_hbm[ic * 128:(ic + 1) * 128, :], in_=hgc[:]),
                        reads=[(hgc, 0), (hgc, 1)], writes=[(hg_hbm, ic)],
                        slot=f'hgw{ic % 2}')
        qs8 = ctx.enter_context(nc.sbuf_tensor("p6_q8", [128, 32], F32))
        for tt in range(8):
            c0, c1 = 1024 + (tt % 4) * 128, 1536 + (tt % 4) * 128
            bld.op(TE, lambda e, tt=tt, c0=c0: e.transpose(
                pmA[:, c0:c0 + 128], acs[:, tt * 128:(tt + 1) * 128], eyef[:]),
                reads=[(acs, 0), eyef], writes=[(pmA, "A2")])
            bld.op(TE, lambda e, tt=tt, c1=c1: e.transpose(
                pmA[:, c1:c1 + 128], acm[:, tt * 128:(tt + 1) * 128], eyef[:]),
                reads=[(acm, 0), eyef], writes=[(pmA, "A3")])
            bld.op(VE, lambda e, tt=tt, c0=c0: e.reduce_sum(
                out=qs8[:, tt:tt + 1], in_=pmA[:, c0:c0 + 128],
                axis=AX.X), reads=[(pmA, "A2")], writes=[(qs8, f"s{tt}")])
            bld.op(VE, lambda e, tt=tt, c1=c1: e.reduce_max(
                out=qs8[:, 8 + tt:9 + tt], in_=pmA[:, c1:c1 + 128],
                axis=AX.X), reads=[(pmA, "A3")], writes=[(qs8, f"m{tt}")])
        qsrd = [(qs8, f"s{tt}") for tt in range(8)]
        qmrd = [(qs8, f"m{tt}") for tt in range(8)]
        bld.op(SC, lambda e: e.activation(
            out=qs8[:, 0:8], in_=qs8[:, 0:8], func=AF.Sqrt,
            bias=epsc[:, 1:2], scale=epsc[:, 2:3]),
            reads=qsrd + [epsc], writes=[(qs8, "rms2")])
        bld.op(SC, lambda e: e.activation(
            out=qs8[:, 8:16], in_=qs8[:, 8:16], func=AF.Sqrt),
            reads=qmrd, writes=[(qs8, "mxa")])
        bld.op(VE, lambda e: e.tensor_scalar(
            out=qs8[:, 16:24], in0=qs8[:, 0:8], scalar1=SQI, scalar2=None,
            op0=ALU.mult), reads=[(qs8, "rms2")], writes=[(qs8, "iden2")])
        bld.op(VE, lambda e: e.reciprocal(out=qs8[:, 16:24], in_=qs8[:, 16:24]),
               reads=[(qs8, "iden2")], writes=[(qs8, "iden2")])
        bld.op(VE, lambda e: e.tensor_mul(a1col[:, 8:16], qs8[:, 8:16], qs8[:, 16:24]),
               reads=[(qs8, "mxa"), (qs8, "iden2")],
               writes=[(a1col, "mm2pre")])
        bld.op(VE, lambda e: e.tensor_scalar(
            out=a1col[:, 8:16], in0=a1col[:, 8:16], scalar1=1e-5, scalar2=None,
            op0=ALU.max), reads=[(a1col, "mm2pre")],
            writes=[(a1col, f"mm2_{tt}") for tt in range(8)])
        mmrd = [(a1col, f"mm2_{tt}") for tt in range(8)]
        bld.op(VE, lambda e: e.reciprocal(out=qs8[:, 24:32], in_=a1col[:, 8:16]),
               reads=mmrd, writes=[(qs8, "c2c")])
        bld.op(VE, lambda e: e.tensor_scalar(
            out=qs8[:, 24:32], in0=qs8[:, 24:32], scalar1=127.0, scalar2=None,
            op0=ALU.mult), reads=[(qs8, "c2c")], writes=[(qs8, "c2c")])
        bld.op(VE, lambda e: e.tensor_mul(qs8[:, 24:32], qs8[:, 24:32], qs8[:, 16:24]),
               reads=[(qs8, "c2c"), (qs8, "iden2")], writes=[(qs8, "c2c")])
        for tt in range(8):
            bk = "A0" if tt < 4 else "A1"
            bld.op(TE, lambda e, tt=tt, bk=bk: e.matmul(
                pmA[0:1, tt * 128:(tt + 1) * 128], qs8[:, 24 + tt:25 + tt],
                eyef[:, 0:128], start=True, stop=True),
                reads=[(qs8, "c2c"), eyef], writes=[(pmA, bk)])
        bld.op(VE, lambda e: e.tensor_copy(rows[0:1, 7 * TOWN:8 * TOWN], pmA[0:1, 0:TOWN]),
               reads=[(pmA, "A0"), (pmA, "A1")], writes=[(rows, "c2r")])
        for tc in range(2):
            bk = f"A{tc}"
            bld.op(TE, lambda e, tc=tc, bk=bk: e.matmul(
                pm[tc][:, 0:512], onesf[0:1, 0:128],
                rows[0:1, 7 * TOWN + tc * 512:7 * TOWN + (tc + 1) * 512],
                start=True, stop=True),
                reads=[(rows, "c2r"), onesf], writes=[(pmA, bk)])
            bld.op(VE, lambda e, tc=tc: e.tensor_copy(
                c2b[:, tc * 512:(tc + 1) * 512], pm[tc][:, 0:512]),
                reads=[(pmA, bk)], writes=[(c2b, 0)])
        bld.cut()
        with ExitStack() as cR:
            hgb4 = [cR.enter_context(nc.sbuf_tensor(f"p7_hg{i}", [128, TOWN], F32))
                    for i in range(4)]
            vb2 = [cR.enter_context(nc.sbuf_tensor(f"p7_v{i}", [128, TOWN], F32))
                   for i in range(2)]
            intb2 = [cR.enter_context(nc.sbuf_tensor(f"p7_int{i}", [128, TOWN], I32))
                     for i in range(2)]
            tb2 = [cR.enter_context(nc.sbuf_tensor(f"p7_t{i}", [128, TOWN], F32))
                   for i in range(2)]
            k2c2 = [cR.enter_context(nc.sbuf_tensor(f"p7_k2{i}", [128, TOWN], BF16))
                    for i in range(2)]

            def hgr(ic):
                hgb = hgb4[ic % 4]
                bld.op(SY, lambda e, ic=ic, hgb=hgb: e.dma_start(
                    out=hgb[:], in_=hg_hbm[ic * 128:(ic + 1) * 128, :]),
                    reads=[(hg_hbm, ic)], writes=[(hgb, 0)], slot=f'hgr{ic % 4}')

            def vmult(ic):
                hgb, vb = hgb4[ic % 4], vb2[ic % 2]
                bld.op(VE, lambda e, hgb=hgb, vb=vb: e.tensor_mul(
                    vb[:], hgb[:], c2b[:]),
                    reads=[(hgb, 0), (c2b, 0)], writes=[(vb, 0)])

            for ic in range(3):
                hgr(ic)
            vmult(0)
            for ic in range(64):
                hgb, vb = hgb4[ic % 4], vb2[ic % 2]
                intb, tb, k2c = intb2[ic % 2], tb2[ic % 2], k2c2[ic % 2]
                if ic + 1 < 64:
                    vmult(ic + 1)
                if debug:
                    bld.op(SY, lambda e, ic=ic, hgb=hgb: e.dma_start(
                        out=dbg["S1"][ic * 128:(ic + 1) * 128, :], in_=hgb[:]),
                        reads=[(hgb, 0)], slot=f'dbgB{ic % 2}')
                bld.op(SC, lambda e, vb=vb, intb=intb: e.activation(
                    out=intb[:], in_=vb[:], func=AF.Copy, bias=256.50002),
                    reads=[(vb, 0)], writes=[(intb, 0)])
                bld.op(VE, lambda e, vb=vb, intb=intb, tb=tb: e.scalar_tensor_tensor(
                    out=tb[:], in0=vb[:], scalar=1.0, in1=intb[:],
                    op0=ALU.mult, op1=ALU.subtract),
                    reads=[(vb, 0), (intb, 0)], writes=[(tb, 0)])
                bld.op(VE, lambda e, tb=tb, vb=vb: e.tensor_scalar(
                    out=vb[:], in0=tb[:], scalar1=-256.5, scalar2=None, op0=ALU.is_gt),
                    reads=[(tb, 0)], writes=[(vb, 0)])
                bld.op(VE, lambda e, tb=tb, vb=vb: e.scalar_tensor_tensor(
                    out=tb[:], in0=tb[:], scalar=-255.5, in1=vb[:],
                    op0=ALU.is_gt, op1=ALU.add),
                    reads=[(tb, 0), (vb, 0)], writes=[(tb, 0)])
                bld.op(VE, lambda e, tb=tb, intb=intb, k2c=k2c: e.scalar_tensor_tensor(
                    out=k2c[:], in0=tb[:], scalar=257.0, in1=intb[:],
                    op0=ALU.subtract, op1=ALU.add),
                    reads=[(tb, 0), (intb, 0), (k2c, 0)], writes=[(k2c, 0)])
                if ic + 3 < 64:
                    hgr(ic + 3)
                bld.op(SY, lambda e, ic=ic, k2c=k2c: e.dma_start(
                    out=k2T_hbm[ic * 128:(ic + 1) * 128, :], in_=k2c[:]),
                    reads=[(k2c, 0)], writes=[(k2T_hbm, ic)], slot=f'k2w{ic % 2}')
                if debug:
                    bld.op(SY, lambda e, ic=ic, k2c=k2c: e.dma_start(
                        out=dbg["k2T"][ic * 128:(ic + 1) * 128, :], in_=k2c[:]),
                        reads=[(k2c, 0)], slot=f'dbgC{ic % 2}')


def _p8(nc, bld, dbg, debug, L):
    alphas, a1col, onesf, eyef, bsel = L["alphas"], L["a1col"], L["onesf"], L["eyef"], L["bsel"]
    pmA, pmB = L["pmA"], L["pmB"]
    w2T_hbm, s_hbm, k2T_hbm = L["w2T_hbm"], L["s_hbm"], L["k2T_hbm"]
    x_hbm, out_hbm, b2B_hbm = L["x_hbm"], L["out_hbm"], L["b2B_hbm"]
    GI = 8
    NG = 64 // GI
    with ExitStack() as ctx:
        S2 = ctx.enter_context(nc.sbuf_tensor("p8_S2", [128, 16 * TOWN], F32))
        ctxM = ctx.enter_context(ExitStack())
        w2b = [ctxM.enter_context(nc.sbuf_tensor(f"p8_w{i}", [128, 2048], F32)) for i in range(2)]
        s2b2 = [ctxM.enter_context(nc.sbuf_tensor(f"p8_s2_{i}", [128, GI * 2048], BF16))
                for i in range(2)]
        k2g = [ctxM.enter_context(nc.sbuf_tensor(f"p8_k2g{i}", [128, GI * TOWN], BF16))
               for i in range(2)]
        for gi in range(NG):
            kg = k2g[gi % 2]
            s2b = s2b2[gi % 2]
            bld.op(SY, lambda e, gi=gi, kg=kg: e.dma_start(
                out=kg.rearrange("p (k c) -> p k c", k=GI),
                in_=k2T_hbm[gi * GI * 128:(gi + 1) * GI * 128, :]
                .rearrange("(k p) c -> p k c", p=128)),
                writes=[(kg, 0)], reads=[(k2T_hbm, c) for c in range(gi * GI, (gi + 1) * GI)],
                slot=f'k2g{gi % 2}')
            for ic in range(GI):
                icg = gi * GI + ic
                wb = w2b[ic % 2]
                bld.op(SY, lambda e, icg=icg, wb=wb: e.dma_start(
                    out=wb[:], in_=w2T_hbm[icg * 128:(icg + 1) * 128, :]),
                    writes=[(wb, 0)], slot=f'w2s{icg % 2}')
                bld.op(SC, lambda e, wb=wb, ic=ic, s2b=s2b: e.activation(
                    out=s2b[:, ic * 2048:(ic + 1) * 2048], in_=wb[:], func=AF.Sign),
                    reads=[(wb, 0)], writes=[(s2b, ic)])
                bld.op(SC, lambda e, wb=wb, icg=icg: e.activation(
                    out=wb[:], in_=wb[:], func=AF.Abs,
                    accum_out=alphas[:, 16 + icg:17 + icg]),
                    reads=[(wb, 0)], writes=[(wb, 0), (alphas, f"w2a{icg}")])
            for ds in range(16):
                pb = pmA[:, (ds % 2) * 1024:(ds % 2 + 1) * 1024]
                for ic in range(GI):
                    for tc in range(2):
                        bld.op(TE, lambda e, ic=ic, ds=ds, tc=tc, pb=pb, kg=kg, s2b=s2b: e.matmul(
                            pb[:, tc * 512:(tc + 1) * 512],
                            s2b[:, ic * 2048 + ds * 128:ic * 2048 + (ds + 1) * 128],
                            kg[:, ic * TOWN + tc * 512:ic * TOWN + (tc + 1) * 512],
                            start=(ic == 0), stop=(ic == GI - 1)),
                            reads=[(s2b, ic), (kg, 0)],
                            writes=[(pmA, f"A{(ds % 2) * 2}"), (pmA, f"A{(ds % 2) * 2 + 1}")],
                            inc=(1 if (ic == GI - 1 and tc == 1) else 0))
                sl = S2[:, ds * TOWN:(ds + 1) * TOWN]
                prd = [(pmA, f"A{(ds % 2) * 2}"), (pmA, f"A{(ds % 2) * 2 + 1}")]
                if gi == 0:
                    bld.op(VE, lambda e, sl=sl, pb=pb: e.tensor_copy(sl, pb),
                           reads=prd, writes=[(S2, ds)])
                else:
                    bld.op(VE, lambda e, sl=sl, pb=pb: e.tensor_add(sl, sl, pb),
                           reads=prd + [(S2, ds)], writes=[(S2, ds)])
        ard = [(alphas, f"w2a{icg}") for icg in range(64)]
        bld.op(VE, lambda e: e.reduce_sum(out=alphas[:, 81:82], in_=alphas[:, 16:80], axis=AX.X),
               reads=ard, writes=[(alphas, "rs2")])
        bld.op(TE, lambda e: e.matmul(pmB[0:1, 0:1], onesf[:, 0:1], alphas[:, 81:82],
                                      start=True, stop=True),
               reads=[(alphas, "rs2"), onesf], writes=[(pmB, "B0")])
        bld.op(VE, lambda e: e.tensor_scalar(
            out=alphas[0:1, 83:84], in0=pmB[0:1, 0:1], scalar1=1.0 / (INNER * D),
            scalar2=None, op0=ALU.mult), reads=[(pmB, "B0")], writes=[(alphas, "al2s")])
        bld.op(TE, lambda e: e.matmul(pmB[:, 4:5], onesf[0:1, 0:128], alphas[0:1, 83:84],
                                      start=True, stop=True),
               reads=[(alphas, "al2s"), onesf], writes=[(pmB, "B0")])
        bld.op(VE, lambda e: e.tensor_scalar(
            out=a1col[:, 8:16], in0=a1col[:, 8:16], scalar1=pmB[:, 4:5], scalar2=1.0 / 127.0,
            op0=ALU.mult, op1=ALU.mult),
            reads=[(pmB, "B0")] + [(a1col, f"mm2_{tt}") for tt in range(8)],
            writes=[(a1col, "a2")])
        if debug:
            bld.op(SY, lambda e: e.dma_start(out=dbg["alpha"][0:1, 1:2], in_=alphas[0:1, 83:84]),
                   reads=[(alphas, "al2s")], slot='dbgD')
        bld.cut()
        ctxM.close()

        with ExitStack() as c2x:
            b2B = c2x.enter_context(nc.sbuf_tensor("p8_b2B", [128, D], F32))
            stile = c2x.enter_context(nc.sbuf_tensor("p8_st", [128, D], F32))
            xtile = c2x.enter_context(nc.sbuf_tensor("p8_xt", [128, D], F32))
            ogt = c2x.enter_context(nc.sbuf_tensor("p8_og", [128, D], F32))
            fblk = c2x.enter_context(nc.sbuf_tensor("p8_fb", [128, 256], F32))
            ptr = [pmA[:, 0:128], pmA[:, 512:640], pmA[:, 1024:1152]]
            bld.op(SY, lambda e: e.dma_start(out=b2B[:], in_=b2B_hbm), writes=[b2B], slot='b2B')
            for j in range(8):
                a2 = a1col[:, 8 + j:9 + j]
                bld.op(SY, lambda e, j=j: e.dma_start(
                    out=stile[:], in_=s_hbm[j * 128:(j + 1) * 128, :]),
                    writes=[(stile, 0)], reads=[(s_hbm, c) for c in range(16)], slot='sgr')
                bld.op(SY, lambda e, j=j: e.dma_start(
                    out=xtile[:], in_=x_hbm[TOWN + j * 128:TOWN + (j + 1) * 128, :]),
                    writes=[(xtile, 0)], slot='xtl')
                for ds in range(16):
                    pb = ptr[ds % 3]
                    bld.op(TE, lambda e, j=j, ds=ds, pb=pb: e.transpose(
                        pb, S2[:, ds * TOWN + j * 128:ds * TOWN + (j + 1) * 128], eyef[:]),
                        reads=[(S2, ds), eyef], writes=[(pmA, f"A{ds % 3}")])
                    fb = fblk[:, (ds % 2) * 128:(ds % 2 + 1) * 128]
                    bld.op(VE, lambda e, a2=a2, ds=ds, pb=pb, fb=fb: e.scalar_tensor_tensor(
                        out=fb, in0=pb, scalar=a2, in1=b2B[:, ds * 128:(ds + 1) * 128],
                        op0=ALU.mult, op1=ALU.add),
                        reads=[(pmA, f"A{ds % 3}"), b2B, (a1col, "a2")],
                        writes=[(fblk, ds % 2)])
                    bld.op(SC, lambda e, ds=ds, fb=fb: e.activation(
                        out=ogt[:, ds * 128:(ds + 1) * 128], in_=fb, func=AF.Silu),
                        reads=[(fblk, ds % 2)], writes=[(ogt, ds)])
                ogt_all = [(ogt, ds) for ds in range(16)]
                if debug:
                    bld.op(SY, lambda e, j=j: e.dma_start(
                        out=dbg["gfm"][j * 128:(j + 1) * 128, 0:D], in_=ogt[:]),
                        reads=ogt_all, slot='dbgE')
                bld.op(VE, lambda e: e.tensor_mul(ogt[:], ogt[:], stile[:]),
                       reads=[(stile, 0)] + ogt_all, writes=ogt_all)
                bld.op(VE, lambda e: e.tensor_add(ogt[:], ogt[:], xtile[:]),
                       reads=[(xtile, 0)] + ogt_all, writes=ogt_all)
                bld.op(SY, lambda e, j=j: e.dma_start(
                    out=out_hbm[j * 128:(j + 1) * 128, :], in_=ogt[:]),
                    reads=ogt_all, slot='outw')



def host_inputs(full):
    x = np.asarray(full["L0_attn_x"], np.float32)
    Wx = np.ascontiguousarray(np.asarray(full["L0_attn_Wx"], np.float32))
    Wg = np.ascontiguousarray(np.asarray(full["L0_attn_Wg"], np.float32))
    bg = np.asarray(full["L0_attn_bg"], np.float32)
    Wo = np.ascontiguousarray(np.asarray(full["L0_attn_Wo"], np.float32))
    A = np.asarray(full["L0_attn_A"], np.float32)
    Dp = np.asarray(full["L0_attn_Dp"], np.float32)
    w1 = np.asarray(full["L0_ffn_w1"], np.float32)
    b1 = np.asarray(full["L0_ffn_b1"], np.float32)
    w2 = np.asarray(full["L0_ffn_w2"], np.float32)
    b2 = np.asarray(full["L0_ffn_b2"], np.float32)

    At = np.ascontiguousarray(A.T)
    Dpc = np.ascontiguousarray(np.repeat(Dp, HD).reshape(16, 128).T)
    bgc = np.ascontiguousarray(bg.reshape(16, 128).T)
    w1T = np.ascontiguousarray(w1.T)
    w2T = np.ascontiguousarray(w2.T)
    b1B = np.ascontiguousarray(b1.reshape(64, 128).T)
    b2B = np.ascontiguousarray(np.broadcast_to(b2, (128, D)))
    Est = np.zeros((H, 128, 32), np.float32)
    for h in range(H):
        Est[h, :, h] = 1
    selst = np.zeros((16, 64, 128), ml_dtypes.bfloat16)
    for c in range(16):
        for p in range(128):
            h = (c * 128 + p) // HD
            selst[c, h % 32, p] = 1
            selst[c, 32 + h % 32, p] = 1
    bsel = np.zeros((H, 32, 128), ml_dtypes.bfloat16)
    for h in range(H):
        bsel[h, h, :] = 1
    eyeb = np.eye(128).astype(ml_dtypes.bfloat16)
    eyef = np.eye(128).astype(np.float32)
    epsc = np.ascontiguousarray(np.broadcast_to(
        np.array([1e-5, 1e-6, 1.0 / INNER, 0.5, 0.0, 0.0, 0.0, 0.0], np.float32), (128, 8)))

    shared = dict(Wx=Wx, Wg=Wg, bgc=bgc, Wo=Wo, At=At, Dpc=Dpc, w1T=w1T,
                  b1B=b1B, w2T=w2T, b2B=b2B, Est=Est, selst=selst, bsel=bsel,
                  eyeb=eyeb, eyef=eyef, epsc=epsc)
    in_maps = []
    for c in range(NCORES):
        b, gg = c // 2, c % 2
        if gg == 1:
            xs = np.ascontiguousarray(x[b])
            mb = np.zeros((H, T), np.float32)
        else:
            xs = np.ascontiguousarray(
                np.concatenate([np.zeros((TOWN, D), np.float32), x[b, :TOWN]], axis=0))
            mb = np.concatenate(
                [np.full((H, TOWN), -40.0, np.float32), np.zeros((H, TOWN), np.float32)], axis=1)
        m = dict(shared)
        m["x_sh"] = xs
        m["maskB"] = np.ascontiguousarray(mb)
        in_maps.append(m)
    return in_maps


def assemble(results):
    out = np.empty((B_, T, D), np.float32)
    for c in range(NCORES):
        b, gg = c // 2, c % 2
        out[b, gg * TOWN:(gg + 1) * TOWN, :] = results[c]["out"]
    return out


def kernel(**inputs):
    in_maps = host_inputs(inputs)
    nc, _ = build_program(debug=False)
    res = run_bass_kernel_spmd(nc, in_maps, core_ids=list(range(NCORES)))
    return assemble(res.results)
